# version 01
# baseline (speedup 1.0000x reference)
import numpy as np
import ml_dtypes
import concourse.bass as bass
import concourse.mybir as mybir
from concourse.bass_utils import run_bass_kernel_spmd
from concourse.bass_core_rust import IndirectOffsetOnAxis

BF16 = mybir.dt.bfloat16
F32 = mybir.dt.float32
I32 = mybir.dt.int32
ALU = mybir.AluOpType
AF = mybir.ActivationFunctionType
AXX = mybir.AxisListType.X
BF = ml_dtypes.bfloat16

NCORE = 8
NTOK = 1024
D = 2048
NCH = 16
E_TOT = 64
E_OWN = 8
CAP = 1024
TOPK = 6
MT = 8832
NMC = MT // 128
LN_EPS = 1e-5
RMS_EPS = 1.1920929e-07
BIG = 1.0e7
BIG2 = 16384.0

ENGINES = ("sync", "vector", "scalar", "tensor", "gpsimd")
ESEM = {"sync": "sq", "vector": "sv", "scalar": "ss", "tensor": "st", "gpsimd": "sg"}


class PB:
    def __init__(self):
        self.ops = {e: [] for e in ENGINES}
        self.semnames = []
        self.cnt = {}
        self.bregs = {}

    def sem(self, name):
        self.semnames.append(name)
        self.cnt[name] = 0

    def wt(self, eng, sem, val):
        if sem in ("dq", "dg", "dsc"):
            val = self.cnt[sem]
        if val and val > 0:
            self.ops[eng].append(lambda e, S, sem=sem, val=val: e.wait_ge(S[sem], val))

    def _do(self, eng, fn, inc, n):
        if inc is None:
            self.ops[eng].append(fn)
            return self.cnt[ESEM[eng]]
        self.cnt[inc] += n
        tag = self.cnt[inc]
        self.ops[eng].append(lambda e, S, fn=fn, inc=inc, n=n: fn(e, S).then_inc(S[inc], n))
        return tag

    def op(self, eng, fn, bar=False, inc="auto"):
        sem = ESEM[eng] if inc == "auto" else inc
        tag = self._do(eng, fn, sem, 1)
        if bar and sem is not None:
            self.wt(eng, sem, tag)
        return tag

    def dma(self, eng, fn, sem=None):
        if sem is None:
            sem = {"sync": "dq", "gpsimd": "dg", "scalar": "dsc"}[eng]
        return self._do(eng, fn, sem, 16)

    def cc(self, fn, sem):
        assert self.cnt[sem] == 0
        self.cnt[sem] = 1
        self.ops["gpsimd"].append(lambda e, S, fn=fn, sem=sem: fn(e, S).then_inc(S[sem]))
        return 1

    def tag(self, key):
        return self.cnt[key]

    _QISS = {"dq": "sync", "dg": "gpsimd", "dsc": "scalar"}
    _QFS = {"dq": "sq", "dg": "sg", "dsc": "ss"}

    def wtd(self, eng, qsem, tag_ignored=None):
        issuer = self._QISS[qsem]
        if eng == issuer:
            self.wt(eng, qsem, self.cnt[qsem])
            return
        if not hasattr(self, "_fcache"):
            self._fcache = {}
        cur = self.cnt[qsem]
        key = (qsem, cur)
        if key not in self._fcache:
            fs = self._QFS[qsem]
            self.wt(issuer, qsem, cur)
            self._fcache[key] = self.op(issuer, lambda e, S: e.nop(), inc=fs)
        self.wt(eng, self._QFS[qsem], self._fcache[key])

    def breg(self, e, S, name, val):
        if name not in self.bregs:
            self.bregs[name] = e.to_reg(val)
        return self.bregs[name]


class Regions:
    def __init__(self, tensor, ncols):
        self.t, self.n, self.map = tensor, ncols, {}

    def alloc(self, name, cols, at=None):
        if at is None:
            at = max((a + c for (a, c) in self.map.values()), default=0)
        assert at + cols <= self.n, f"{name}: {at}+{cols} > {self.n}"
        self.map[name] = (at, cols)
        return at

    def clear(self, keep=()):
        self.map = {k: v for k, v in self.map.items() if k in keep}

    def __call__(self, name, lo=0, hi=None, p0=0, p1=128):
        a, c = self.map[name]
        if hi is None:
            hi = c
        assert 0 <= lo <= hi <= c, (name, lo, hi, c)
        return self.t[p0:p1, a + lo:a + hi]


def TT(out, a, b, op):
    return lambda e, S: e.tensor_tensor(out=out, in0=a, in1=b, op=op)


def TS(out, a, s1, op0, s2=None, op1=None):
    def f(e, S):
        kw = {}
        if op1 is not None:
            kw["scalar2"] = s2
            kw["op1"] = op1
        else:
            kw["scalar2"] = None
        return e.tensor_scalar(out=out, in0=a, scalar1=s1, op0=op0, **kw)
    return f


def STT(out, a, s, b, op0, op1):
    return lambda e, S: e.scalar_tensor_tensor(out=out, in0=a, scalar=s, in1=b,
                                               op0=op0, op1=op1)


def CP(out, a):
    return lambda e, S: e.tensor_copy(out=out, in_=a)


def SCP(out, a):
    return lambda e, S: e.copy(out=out, in_=a)


def RED(out, a, op):
    return lambda e, S: e.tensor_reduce(out=out, in_=a, axis=AXX, op=op)


def ACT(out, a, f, bias=None, scale=None):
    kw = {}
    if bias is not None:
        kw["bias"] = bias
    if scale is not None:
        kw["scale"] = scale
    return lambda e, S: e.activation(out=out, in_=a, func=f, **kw)


def MM(psum, lhsT, rhs, start, stop):
    return lambda e, S: e.matmul(psum, lhsT, rhs, start=start, stop=stop)


def TRS(psum, a, ident, kp=128):
    f = a.shape[-1]
    return lambda e, S: e.transpose(psum[0:f, 0:kp], a, ident[0:kp, 0:kp])


def MEMS(ap, v):
    return lambda e, S: e.memset(ap, v)


def RECIP(out, a):
    return lambda e, S: e.reciprocal(out=out, in_=a)


def SCANOP(out, d0, d1, init, op0=ALU.mult, op1=ALU.add):
    return lambda e, S: e.tensor_tensor_scan(out=out, data0=d0, data1=d1,
                                             initial=init, op0=op0, op1=op1)


def DMA(out, in_, **kw):
    return lambda e, S: e.dma_start(out=out, in_=in_, **kw)


TAPDEFS = {
    "ln1": ((D, NTOK), BF16), "scores": ((128, 512), F32),
    "topk_e": ((128, 48), F32), "topk_w": ((128, 48), F32),
    "lpos": ((128, 48), F32), "counts": ((1, 64), F32),
    "x_sh": ((D, NTOK), F32),
    "gidx": ((128, 8), I32), "xrow": ((128, 8), I32),
    "pdst": ((128, 64), I32), "hrow": ((128, 16), BF16),
    "x1": ((D, NTOK), F32),
    "xn2": ((128, NCH * 1024), BF16), "dtsp": ((16, NTOK), F32),
    "xbar": ((16, NTOK), F32), "yscan": ((16, NTOK), F32),
    "x2": ((D, NTOK), F32), "dbg": ((128, 64), F32),
    "dA2": ((128, 2048), F32), "uf2": ((128, 2048), F32), "sc2": ((128, 2048), F32), "ypr": ((128, 2048), F32),
    "glog": ((64, 1024), F32), "stok": ((128, 512), F32),
}


class Ctx:
    pass


def build_program(taps=(), phases=99):
    nc = bass.Bass(target_bir_lowering=False)
    pb = PB()
    ctx = Ctx()
    ctx.nc, ctx.pb, ctx.taps, ctx.phases = nc, pb, set(taps), phases

    EI = {}

    def ei(name, shape, dt):
        EI[name] = nc.dram_tensor(name, list(shape), dt, kind="ExternalInput").ap()

    ei("x_fm", (D, NTOK), F32)
    ei("wg", (128, 16 * 64), F32)
    ei("wsu", (128, 8 * 2048), F32)
    ei("wsd", (128, 16 * 512), F32)
    ei("wup", (E_OWN * 128, 4 * 2048), F32)
    ei("wdn", (E_OWN * 128, 2 * 2048), F32)
    ei("w_inT", (128, NMC * 2048), F32)
    ei("w_outT", (128, 16 * 4096), F32)
    ei("wh2i", (16, 4096), F32)
    CONSTS = [
        ("ident_f", (128, 128), F32), ("iota_col", (128, 1), F32),
        ("iota64", (128, 64), F32), ("ysum_f", (128, 32), F32),
        ("tok_base", (128, 1), F32), ("eg1024_col", (128, 1), F32),
        ("sel_own_f", (64, 8), F32),
        ("stitch_w", (8, 8), F32), ("stitch_pick", (8, 1), F32),
        ("halo_selm", (32, 4), F32), ("ones_row_f", (1, 128), F32),
        ("ones_col_f", (128, 1), F32), ("eps_ln", (128, 1), F32),
        ("eps_rms", (128, 1), F32), ("one_colf", (128, 1), F32),
        ("wnormc", (128, 16), F32), ("dtb", (16, 1), F32), ("alog", (16, 1), F32),
        ("wconv", (128, 128), F32),
    ]
    CONSTS_B = [
        ("ident_b", (128, 128), BF16), ("ones_col_b", (128, 1), BF16),
        ("ones_row_b", (1, 128), BF16), ("tri_excl_b", (128, 128), BF16),
        ("xmean_sel_b", (128, 512), BF16), ("headbc_b", (16, 256), BF16),
    ]
    for nm, shape, dt in CONSTS + CONSTS_B:
        ei(nm, shape, dt)
    ctx.EI = EI

    ctx.out_t = nc.dram_tensor("out_t", [NTOK, D], F32, kind="ExternalOutput").ap()
    ctx.tap_t = {}
    for t in ctx.taps:
        shape, dt = TAPDEFS[t]
        ctx.tap_t[t] = nc.dram_tensor("tap_" + t, list(shape), dt, kind="ExternalOutput").ap()

    dr = {}

    def di(name, shape, dt, **kw):
        dr[name] = nc.dram_tensor(name, list(shape), dt, **kw).ap()

    di("xfm_b", (D, NTOK), BF16)
    di("tag_in", (NTOK, D), BF16)
    di("tag_out", (NCORE * NTOK, D), BF16, addr_space="Shared")
    di("lloc", (E_TOT * CAP, 2), F32)
    di("lall", (NCORE * E_TOT * CAP, 2), F32, addr_space="Shared")
    di("ctloc", (1, E_TOT), F32)
    di("ctall", (NCORE, E_TOT), F32, addr_space="Shared")
    di("xdisp", (E_OWN * CAP, D), BF16)
    di("pacc0", (NCORE * NTOK, D), BF16)
    di("pacc1", (NCORE * NTOK, D), BF16)
    di("rsout0", (NTOK, D), BF16)
    di("rsout1", (NTOK, D), BF16)
    di("halo_in", (4, D), F32)
    di("halo_all", (NCORE * 4, D), F32, addr_space="Shared")
    di("sum_in", (272, 1), F32)
    di("sum_all", (NCORE * 272, 1), F32, addr_space="Shared")
    ctx.dr = dr

    for s in ("sv", "ss", "st", "sg", "sq", "dq", "dg", "dsc",
              "cc_tag0", "cc_ct0", "cc_l0", "cc_tag1", "cc_ct1", "cc_l1",
              "cc_halo", "cc_sum", "cc_rs0", "cc_rs1"):
        pb.sem(s)

    import contextlib
    stack = contextlib.ExitStack()
    sb = {}
    for nm, shape, dt, kw in [
        ("xs", (128, NCH * 1024), F32, {}),
        ("ab", (128, 45248), BF16, {"is_transpose_dest": True}),
        ("af", (128, 12544), F32, {}),
        ("cbf", (128, 640), F32, {}),
        ("cbb", (128, 1216), BF16, {}),
        ("ib", (128, 144), I32, {}),
    ]:
        sb[nm] = stack.enter_context(nc.sbuf_tensor(nm, list(shape), dt, **kw))
    ps = {}
    for nm, shape, dt in [
        ("p0", (128, 512), F32), ("p1", (128, 512), F32),
        ("p2", (128, 512), F32), ("p3", (128, 512), F32),
        ("p4", (128, 512), F32), ("p5", (128, 512), F32),
        ("pt", (128, 128), F32), ("ptb", (128, 128), BF16),
    ]:
        ps[nm] = stack.enter_context(nc.psum_tensor(nm, list(shape), dt))
    semh = {nm: stack.enter_context(nc.semaphore(nm)) for nm in pb.semnames}

    ctx.xs = sb["xs"]
    ctx.AB = Regions(sb["ab"], 45248)
    ctx.AF = Regions(sb["af"], 12544)
    ctx.cbf, ctx.cbb, ctx.ibuf = sb["cbf"], sb["cbb"], sb["ib"]
    ctx.ps = ps
    ctx.P = [ps[f"p{i}"] for i in range(6)]
    ctx.pt, ctx.ptb = ps["pt"], ps["ptb"]

    ctx.CF, ctx.CB = {}, {}
    col = 0
    for nm, shape, dt in CONSTS:
        ctx.CF[nm] = (col, shape[0], shape[1])
        col += shape[1]
    ctx.CF["zero_col"] = (col, 128, 1); col += 1
    assert col <= 640, col
    col = 0
    for nm, shape, dt in CONSTS_B:
        ctx.CB[nm] = (col, shape[0], shape[1])
        col += shape[1]
    ctx.CB["zero_colb"] = (col, 128, 1); col += 1
    assert col <= 1216, col

    emit_program(ctx)

    with stack:
        with nc.Block() as block:
            S = dict(semh)
            for eng in ENGINES:
                getattr(block, eng)(lambda e, _ops=pb.ops[eng], _S=S: [f(e, _S) for f in _ops])
    return nc


def cf(ctx, nm):
    a, r, c = ctx.CF[nm]
    return ctx.cbf[0:r, a:a + c]


def cb(ctx, nm):
    a, r, c = ctx.CB[nm]
    return ctx.cbb[0:r, a:a + c]


def emit_program(ctx):
    emit_p0(ctx)
    if ctx.phases >= 1:
        emit_ffn(ctx, app=0)
    if ctx.phases >= 2:
        emit_mamba(ctx)
    if ctx.phases >= 3:
        emit_ffn(ctx, app=1)
    if ctx.phases >= 4:
        emit_output(ctx)
    emit_final_wait(ctx)


def emit_p0(ctx):
    pb, EI, xs = ctx.pb, ctx.EI, ctx.xs
    for nm, (a, r, c) in ctx.CF.items():
        if nm == "zero_col":
            continue
        pb.dma("sync", DMA(ctx.cbf[0:r, a:a + c], EI[nm][:, :]))
    for nm, (a, r, c) in ctx.CB.items():
        if nm == "zero_colb":
            continue
        pb.dma("sync", DMA(ctx.cbb[0:r, a:a + c], EI[nm][:, :]))
    for c in range(NCH):
        pb.dma("sync", DMA(xs[:, c * 1024:(c + 1) * 1024],
                           EI["x_fm"][c * 128:(c + 1) * 128, :]))
    ctx.t_consts = pb.tag("dq")
    pb.op("vector", MEMS(cf(ctx, "zero_col"), 0.0))
    pb.op("vector", MEMS(cb(ctx, "zero_colb"), 0.0))
    ctx.AB.clear()
    ctx.AB.alloc("zeros", 2048, at=16384)
    tz = pb.op("vector", MEMS(ctx.AB("zeros"), 0.0))
    pb.wt("sync", "sv", tz)
    pb.wt("gpsimd", "sv", tz)
    pb.dma("gpsimd", DMA(ctx.dr["lloc"].rearrange("(p a) c -> p (a c)", p=128),
                         ctx.AB("zeros")[:, 0:1024]))
    napps = 2 if ctx.phases >= 3 else 1
    for app in range(napps):
        pa = ctx.dr[f"pacc{app}"]
        for i in range(64):
            pb.dma("sync", DMA(pa[i * 128:(i + 1) * 128, :], ctx.AB("zeros")))
    ctx.t_pzero = pb.tag("dq")



def emit_ln(ctx, out_region, wait_tags=(), rms_fuse=False):
    pb, xs, AB, AF_ = ctx.pb, ctx.xs, ctx.AB, ctx.AF
    P = ctx.P
    for eng in ("vector", "scalar", "tensor"):
        for (s, t) in wait_tags:
            if s in ("dq", "dg", "dsc"):
                pb.wtd(eng, s, t)
            else:
                pb.wt(eng, s, t)
    mm_tags = []
    for c in range(NCH):
        xc = xs[:, c * 1024:(c + 1) * 1024]
        xb = AB(out_region, c * 1024, (c + 1) * 1024)
        sq = AB("ebx", (c % 2) * 1024, (c % 2) * 1024 + 1024)
        if c >= 2:
            pb.wt("scalar", "st", mm_tags[c - 2])
        tv = pb.op("vector", CP(xb, xc))
        tsc = pb.op("scalar", ACT(sq, xc, AF.Square))
        pb.wt("tensor", "sv", tv)
        pb.wt("tensor", "ss", tsc)
        for h in (0, 1):
            pb.op("tensor", MM(P[h][0:1, 0:512], cb(ctx, "ones_col_b"),
                               xb[:, h * 512:(h + 1) * 512], c == 0, c == NCH - 1),
                  inc=(None if c < NCH - 1 else "auto"))
            t = pb.op("tensor", MM(P[2 + h][0:1, 0:512], cb(ctx, "ones_col_b"),
                                   sq[:, h * 512:(h + 1) * 512], c == 0, c == NCH - 1))
        mm_tags.append(t)
    t_sums = pb.tag("st")
    sr = lambda lo, hi: AF_("srow", lo, hi, p1=1)
    pb.wt("vector", "st", t_sums)
    pb.op("vector", CP(sr(0, 512), P[0][0:1, 0:512]))
    pb.op("vector", CP(sr(512, 1024), P[1][0:1, 0:512]), bar=True)
    pb.op("vector", TS(sr(1024, 2048), sr(0, 1024), 1.0 / D, ALU.mult), bar=True)
    pb.op("vector", CP(sr(0, 512), P[2][0:1, 0:512]))
    pb.op("vector", CP(sr(512, 1024), P[3][0:1, 0:512]), bar=True)
    pb.op("vector", TT(sr(2048, 3072), sr(1024, 2048), sr(1024, 2048), ALU.mult), bar=True)
    pb.op("vector", STT(sr(2048, 3072), sr(0, 1024), 1.0 / D, sr(2048, 3072),
                        ALU.mult, ALU.subtract), bar=True)
    if rms_fuse:
        pb.wt("scalar", "sv", pb.tag("sv"))
        t = pb.op("scalar", ACT(sr(3072, 4096), sr(2048, 3072), AF.Identity,
                                bias=cf(ctx, "eps_ln")[0:1, :]))
        pb.wt("vector", "ss", t)
        pb.op("vector", RECIP(sr(3072, 4096), sr(3072, 4096)), bar=True)
        pb.op("vector", TT(sr(3072, 4096), sr(2048, 3072), sr(3072, 4096), ALU.mult),
              bar=True)
        pb.wt("scalar", "sv", pb.tag("sv"))
        t = pb.op("scalar", ACT(sr(3072, 4096), sr(3072, 4096), AF.Sqrt,
                                bias=cf(ctx, "eps_rms")[0:1, :]))
        pb.wt("vector", "ss", t)
        pb.op("vector", RECIP(sr(3072, 4096), sr(3072, 4096)), bar=True)
    pb.wt("scalar", "sv", pb.tag("sv"))
    t = pb.op("scalar", ACT(sr(2048, 3072), sr(2048, 3072), AF.Sqrt,
                            bias=cf(ctx, "eps_ln")[0:1, :]))
    pb.wt("vector", "ss", t)
    t_rstd = pb.op("vector", RECIP(sr(2048, 3072), sr(2048, 3072)), bar=True)
    pb.wt("tensor", "sv", t_rstd)
    rows = [(1024, 0), (2048, 1024)] + ([(3072, 2048)] if rms_fuse else [])
    for (src, dst) in rows:
        for h in (0, 1):
            tmm = pb.op("tensor", MM(P[4][:, 0:512], cf(ctx, "ones_row_f"),
                                     sr(src + h * 512, src + h * 512 + 512), True, True))
            pb.wt("vector", "st", tmm)
            tv = pb.op("vector", CP(AF_("stats", dst + h * 512, dst + h * 512 + 512),
                                    P[4][:, 0:512]))
            pb.wt("tensor", "sv", tv)
    t_bc = pb.tag("sv")
    pb.wt("vector", "sv", t_bc)
    return t_bc



def emit_ffn(ctx, app):
    pb, EI, dr = ctx.pb, ctx.EI, ctx.dr
    xs, AB, AF_, ibuf = ctx.xs, ctx.AB, ctx.AF, ctx.ibuf
    P, pt = ctx.P, ctx.pt
    taps = ctx.taps if app == 0 else set()
    tap = ctx.tap_t
    cc_tag, cc_ct, cc_l = f"cc_tag{app}", f"cc_ct{app}", f"cc_l{app}"

    AB.clear()
    AB.alloc("xln", 16384, at=0)
    AB.alloc("tt", 16384, at=16384)
    AB.alloc("ysh", 4096, at=16384)
    AB.alloc("hsh", 4096, at=20480)
    AB.alloc("sg", 1024, at=24576)
    AB.alloc("wa", 2048, at=32768)
    AB.alloc("wb", 2048, at=34816)
    AB.alloc("ebx", 2048, at=36864)
    AB.alloc("gt", 4096, at=36864)
    AB.alloc("hT", 2048, at=40960)
    AB.alloc("drow", 2048, at=43008)
    AF_.clear()
    AF_.alloc("stats", 3072, at=0)
    AF_.alloc("srow", 4096, at=3072)
    AF_.alloc("wgb", 1024, at=7168)
    AF_.alloc("sc", 1024, at=8192)
    AF_.alloc("stok", 512, at=7168)
    AF_.alloc("tkv", 512, at=9216)
    AF_.alloc("ew", 192, at=9728)
    AF_.alloc("posw", 512, at=9920)
    AF_.alloc("mrg", 1024, at=10432)
    AF_.alloc("t1", 1024, at=11456)
    AF_.alloc("t2", 1024, at=10432)

    wait0 = [("dq", ctx.t_consts)] if app == 0 else [("sv", ctx.x2_tag)]
    if app == 1:
        pb.wt("sync", "sv", ctx.x2_tag)
    emit_ln(ctx, "xln", wait_tags=wait0)
    mean_b = AF_("stats", 0, 1024)
    rstd_b = AF_("stats", 1024, 2048)

    t_wg = pb.dma("sync", DMA(AF_("wgb"), EI["wg"][:, :]))
    pb.wtd("tensor", "dq", t_wg)

    for c in range(NCH):
        xc = xs[:, c * 1024:(c + 1) * 1024]
        tc = AF_("t1") if c % 2 == 0 else AF_("t2")
        xnb = AB("xln", c * 1024, (c + 1) * 1024)
        pb.op("vector", TT(tc, xc, mean_b, ALU.subtract), bar=True)
        tv = pb.op("vector", TT(tc, tc, rstd_b, ALU.mult), bar=True)
        pb.wt("scalar", "sv", tv)
        tsc = pb.op("scalar", SCP(xnb, tc))
        pb.wt("tensor", "sv", tv)
        pb.wt("tensor", "ss", tsc)
        for h in (0, 1):
            t = pb.op("tensor", MM(P[4 + h][0:64, 0:512],
                                   AF_("wgb", c * 64, (c + 1) * 64),
                                   tc[:, h * 512:(h + 1) * 512], c == 0, c == NCH - 1),
                      inc=(None if (h == 0 and c < NCH - 1) else "auto"))
        pb.wt("vector", "st", t)
        pb.wt("sync", "ss", tsc)
        pb.dma("sync", DMA(dr["xfm_b"][c * 128:(c + 1) * 128, :], xnb))
    t_lnB = pb.tag("ss")
    t_gate = pb.tag("st")
    t_xfm = pb.tag("dq")
    if "ln1" in taps:
        pb.wt("sync", "ss", t_lnB)
        for c in range(NCH):
            pb.dma("sync", DMA(tap["ln1"][c * 128:(c + 1) * 128, :],
                               AB("xln", c * 1024, (c + 1) * 1024)))

    pb.wt("sync", "dq", t_xfm)
    if app == 0:
        pb.wtd("sync", "dg")
    for i in range(8):
        pb.dma("sync", DMA(AB("tt", i * 2048, (i + 1) * 2048),
                           dr["xfm_b"][:, i * 128:(i + 1) * 128], transpose=True))
    pb.wt("sync", "dq", pb.tag("dq"))
    for i in range(8):
        pb.dma("sync", DMA(dr["tag_in"][i * 128:(i + 1) * 128, :],
                           AB("tt", i * 2048, (i + 1) * 2048)))
    t_tagin = pb.tag("dq")
    pb.wtd("gpsimd", "dq", t_tagin)
    pb.cc(lambda e, S: e.collective_compute(
        "AllGather", ALU.bypass, replica_groups=[list(range(NCORE))],
        ins=[dr["tag_in"].opt()], outs=[dr["tag_out"].opt()]), cc_tag)

    if "glog" in taps:
        pb.wt("vector", "st", t_gate)
        pb.op("vector", CP(AF_("t1", 0, 512, p1=64), P[4][0:64, 0:512]))
        tvg = pb.op("vector", CP(AF_("t1", 512, 1024, p1=64), P[5][0:64, 0:512]), bar=True)
        pb.wt("sync", "sv", tvg)
        pb.dma("sync", DMA(tap["glog"][:, :], AF_("t1", 0, 1024, p1=64)))
    pb.wt("scalar", "st", t_gate)
    for h in (0, 1):
        pb.op("scalar", ACT(AF_("sc", h * 512, (h + 1) * 512, p1=64),
                            P[4 + h][0:64, 0:512], AF.Sigmoid))
    t_sig = pb.tag("ss")
    if "scores" in taps:
        pb.wt("sync", "ss", t_sig)
        pb.dma("sync", DMA(tap["scores"][0:64, :], AF_("sc", 0, 512, p1=64)))
        tq_tap = pb.dma("sync", DMA(tap["scores"][64:128, :], AF_("sc", 512, 1024, p1=64)))
        pb.wtd("vector", "dq", tq_tap)
    pb.wt("tensor", "ss", t_sig)
    for i in range(8):
        tmm = pb.op("tensor", TRS(pt[:, :], AF_("sc", i * 128, (i + 1) * 128, p1=64),
                                  cf(ctx, "ident_f"), kp=64))
        pb.wt("vector", "st", tmm)
        tv = pb.op("vector", CP(AF_("stok", i * 64, (i + 1) * 64), pt[:, 0:64]))
        pb.wt("tensor", "sv", tv)

    if "stok" in taps:
        pb.wt("sync", "sv", pb.tag("sv"))
        pb.dma("sync", DMA(tap["stok"][:, :], AF_("stok", 0, 512)))
    EWE, EWV, EWW = 0, 48, 96
    for i in range(8):
        cur = AF_("tkv", 0, 64)
        pb.op("vector", CP(cur, AF_("stok", i * 64, (i + 1) * 64)), bar=True)
        for k in range(TOPK):
            val = AF_("ew", EWV + i * 6 + k, EWV + i * 6 + k + 1)
            idx = AF_("ew", EWE + i * 6 + k, EWE + i * 6 + k + 1)
            neq = AF_("tkv", 128, 192)
            msk = AF_("tkv", 192, 256)
            pb.op("vector", RED(val, cur, ALU.max), bar=True)
            pb.op("vector", TS(neq, cur, val, ALU.not_equal), bar=True)
            pb.op("vector", STT(msk, neq, 1000.0, cf(ctx, "iota64"), ALU.mult, ALU.add), bar=True)
            pb.op("vector", RED(idx, msk, ALU.min), bar=True)
            if k < TOPK - 1:
                eqm = AF_("tkv", 256, 320)
                pb.op("vector", TS(eqm, cf(ctx, "iota64"), idx, ALU.is_equal), bar=True)
                pb.op("vector", STT(cur, eqm, -1.0e9, cur, ALU.mult, ALU.add), bar=True)
    for i in range(8):
        vs = AF_("ew", EWV + i * 6, EWV + (i + 1) * 6)
        ws = AF_("tkv", 320, 321)
        pb.op("vector", RED(ws, vs, ALU.add), bar=True)
        pb.op("vector", RECIP(ws, ws), bar=True)
        pb.op("vector", TS(AF_("ew", EWW + i * 6, EWW + (i + 1) * 6), vs, ws, ALU.mult), bar=True)
    t_topk = pb.tag("sv")
    if app == 0:
        for nm, base in (("topk_e", EWE), ("topk_w", EWW)):
            if nm in taps:
                pb.wt("sync", "sv", t_topk)
                pb.dma("sync", DMA(tap[nm][:, :], AF_("ew", base, base + 48)))

    pb.wt("tensor", "ss", t_sig)
    pb.wt("gpsimd", "dg", pb.tag("dg"))
    carry_b = AB("ebx", 0, 64, p0=0, p1=1)
    carry_f = AF_("tkv", 448, 512, p1=1)
    pb.op("vector", MEMS(carry_b, 0.0), bar=True)
    pb.op("vector", MEMS(carry_f, 0.0), bar=True)
    prev_sc = pb.tag("dg")
    for i in range(8):
        pb.wtd("vector", "dg", prev_sc)
        oh = AF_("posw", 0, 64)
        for k in range(TOPK):
            ecol = AF_("ew", EWE + i * 6 + k, EWE + i * 6 + k + 1)
            ohk = AF_("posw", 128, 192)
            pb.op("vector", TS(ohk, cf(ctx, "iota64"), ecol, ALU.is_equal), bar=True)
            if k == 0:
                pb.op("vector", CP(oh, ohk), bar=True)
            else:
                pb.op("vector", TT(oh, oh, ohk, ALU.add), bar=True)
        ohb = AB("ebx", 1024, 1088)
        tv = pb.op("vector", CP(ohb, oh), bar=True)
        pb.wt("tensor", "sv", tv)
        pb.op("tensor", MM(P[4][:, 0:64], cb(ctx, "tri_excl_b"), ohb, True, False), inc=None)
        pb.op("tensor", MM(P[4][:, 0:64], cb(ctx, "ones_row_b"), carry_b, False, True))
        tcar = pb.op("tensor", MM(pt[0:1, 0:64], cb(ctx, "ones_col_b"), ohb, True, True))
        pb.wt("vector", "st", tcar)
        posf = AF_("posw", 64, 128)
        pb.op("vector", CP(posf, P[4][:, 0:64]))
        pb.op("vector", TT(carry_f, carry_f, pt[0:1, 0:64], ALU.add), bar=True)
        tvc = pb.op("vector", CP(carry_b, carry_f), bar=True)
        pb.wt("tensor", "sv", tvc)
        tkid = AF_("tkv", 321, 322)
        pb.op("vector", TS(tkid, cf(ctx, "iota_col"), float(i * 128), ALU.add), bar=True)
        pb.op("vector", TT(tkid, tkid, cf(ctx, "tok_base"), ALU.add), bar=True)
        for k in range(TOPK):
            ecol = AF_("ew", EWE + i * 6 + k, EWE + i * 6 + k + 1)
            ohk = AF_("posw", 128, 192)
            pb.op("vector", TS(ohk, cf(ctx, "iota64"), ecol, ALU.is_equal), bar=True)
            tmp = AF_("posw", 192, 256)
            pb.op("vector", TT(tmp, ohk, posf, ALU.mult), bar=True)
            lp = AF_("mrg", i * 6 + k, i * 6 + k + 1)
            pb.op("vector", RED(lp, tmp, ALU.add), bar=True)
            dsti = AF_("tkv", 322, 323)
            pb.op("vector", STT(dsti, ecol, 1024.0, lp, ALU.mult, ALU.add), bar=True)
            icol = ibuf[:, k:k + 1]
            pb.op("vector", CP(icol, dsti))
            pb.op("vector", CP(AF_("tkv", 324 + k * 2, 325 + k * 2), tkid))
            tv = pb.op("vector", CP(AF_("tkv", 325 + k * 2, 326 + k * 2),
                                    AF_("ew", EWW + i * 6 + k, EWW + i * 6 + k + 1)), bar=True)
            pb.wt("gpsimd", "sv", tv)
            _rec = AF_("tkv", 324 + k * 2, 326 + k * 2)
            pb.dma("gpsimd", lambda e, S, icol=icol, _rec=_rec: e.indirect_dma_start(
                out=dr["lloc"], out_offset=IndirectOffsetOnAxis(icol, 0),
                in_=_rec, in_offset=None,
                bounds_check=pb.breg(e, S, "b_lloc", E_TOT * CAP - 1),
                oob_is_err=False))
        prev_sc = pb.tag("dg")
    if "lpos" in taps:
        pb.wt("sync", "sv", pb.tag("sv"))
        pb.dma("sync", DMA(tap["lpos"][:, :], AF_("mrg", 0, 48)))
    tct = pb.op("vector", CP(AF_("tkv", 384, 448, p1=1), carry_f), bar=True)
    pb.wt("sync", "sv", tct)
    tq = pb.dma("sync", DMA(dr["ctloc"][:, :], AF_("tkv", 384, 448, p1=1)))
    if "counts" in taps:
        pb.dma("sync", DMA(tap["counts"][:, :], AF_("tkv", 384, 448, p1=1)))
    pb.wtd("gpsimd", "dq", tq)
    pb.wt("gpsimd", "dg", pb.tag("dg"))
    pb.cc(lambda e, S: e.collective_compute(
        "AllGather", ALU.bypass, replica_groups=[list(range(NCORE))],
        ins=[dr["ctloc"].opt()], outs=[dr["ctall"].opt()]), cc_ct)
    pb.cc(lambda e, S: e.collective_compute(
        "AllGather", ALU.bypass, replica_groups=[list(range(NCORE))],
        ins=[dr["lloc"].opt()], outs=[dr["lall"].opt()]), cc_l)

    emit_shared(ctx, app, t_lnB)
    emit_experts(ctx, app, cc_tag, cc_ct, cc_l)
    emit_routed_add(ctx, app)


def emit_shared(ctx, app, t_ln):
    pb, EI, xs, AB = ctx.pb, ctx.EI, ctx.xs, ctx.AB
    P = ctx.P
    pb.wtd("vector", "dq", pb.tag("dq"))
    pb.wtd("scalar", "dq", pb.tag("dq"))
    mm_done = [0] * 8
    for mc in range(8):
        w = AB("wa") if mc % 2 == 0 else AB("wb")
        if mc >= 2:
            pb.wt("gpsimd", "st", mm_done[mc - 2])
        tq = pb.dma("gpsimd", DMA(w, EI["wsu"][:, mc * 2048:(mc + 1) * 2048]))
        pb.wtd("tensor", "dg", tq)
        pb.wt("tensor", "ss", t_ln)
        for h in (0, 1):
            pp = P[(mc % 2) * 2 + h]
            for kc in range(16):
                t = pb.op("tensor", MM(pp[:, 0:512], w[:, kc * 128:(kc + 1) * 128],
                                       AB("xln", kc * 1024 + h * 512, kc * 1024 + h * 512 + 512),
                                       kc == 0, kc == 15),
                          inc=(None if kc < 15 else "auto"))
        mm_done[mc] = t
        if mc < 4:
            pb.wt("vector", "st", t)
            for h in (0, 1):
                tv = pb.op("vector", CP(AB("ysh", mc * 1024 + h * 512, mc * 1024 + h * 512 + 512),
                                        P[(mc % 2) * 2 + h][:, 0:512]))
            pb.wt("tensor", "sv", tv)
        else:
            g = mc - 4
            pb.wt("scalar", "st", t)
            for h in (0, 1):
                tsc = pb.op("scalar", ACT(AB("sg", h * 512, h * 512 + 512),
                                          P[(mc % 2) * 2 + h][:, 0:512], AF.Silu))
            pb.wt("vector", "ss", tsc)
            tv = pb.op("vector", TT(AB("hsh", g * 1024, (g + 1) * 1024), AB("sg"),
                                    AB("ysh", g * 1024, (g + 1) * 1024), ALU.mult), bar=True)
            pb.wt("tensor", "sv", tv)
            pb.wt("scalar", "sv", tv)
    t_h = pb.tag("sv")
    for mc in range(16):
        w = AB("wa") if mc % 2 == 0 else AB("wb")
        pb.wt("gpsimd", "st", pb.tag("st"))
        tq = pb.dma("gpsimd", DMA(w[:, 0:512], EI["wsd"][:, mc * 512:(mc + 1) * 512]))
        pb.wtd("tensor", "dg", tq)
        pb.wt("tensor", "sv", t_h)
        for h in (0, 1):
            pp = P[4 + h]
            for kc in range(4):
                t = pb.op("tensor", MM(pp[:, 0:512], w[:, kc * 128:(kc + 1) * 128],
                                       AB("hsh", kc * 1024 + h * 512, kc * 1024 + h * 512 + 512),
                                       kc == 0, kc == 3),
                          inc=(None if kc < 3 else "auto"))
        pb.wt("vector", "st", t)
        for h in (0, 1):
            xc = xs[:, mc * 1024 + h * 512:mc * 1024 + h * 512 + 512]
            tv = pb.op("vector", STT(xc, P[4 + h][:, 0:512], 0.5, xc, ALU.mult, ALU.add))
        pb.wt("tensor", "sv", tv)
    ctx.t_shared = pb.tag("sv")
    ctx.t_shared_mm = pb.tag("st")
    taps = ctx.taps if app == 0 else set()
    if "x_sh" in taps:
        pb.wt("sync", "sv", ctx.t_shared)
        for c in range(NCH):
            pb.dma("sync", DMA(ctx.tap_t["x_sh"][c * 128:(c + 1) * 128, :],
                               xs[:, c * 1024:(c + 1) * 1024]))


def emit_experts(ctx, app, cc_tag, cc_ct, cc_l):
    pb, EI, dr = ctx.pb, ctx.EI, ctx.dr
    AB, AF_, ibuf = ctx.AB, ctx.AF, ctx.ibuf
    P, pt = ctx.P, ctx.pt
    taps = ctx.taps if app == 0 else set()
    tap = ctx.tap_t
    pacc = dr[f"pacc{app}"]

    if app == 0:
        pb.op("vector", MEMS(AB("gt", 0, 4096), 0.0), bar=True)
        pb.op("vector", MEMS(AF_("mrg", 512, 640), 0.0), bar=True)
    pb.wt("sync", cc_ct, 1)
    _ctdst = AF_("mrg", 64, 72, p1=64)
    tq = pb.dma("sync", lambda e, S, _ctdst=_ctdst: e.dma_start(
        out=_ctdst,
        in_=dr["ctall"].rearrange("r e -> e r"), allow_slow_non_contiguous=True))
    pb.wtd("tensor", "dq", tq)
    t = pb.op("tensor", MM(pt[0:8, 0:8], cf(ctx, "sel_own_f"), AF_("mrg", 64, 72, p1=64),
                           True, True))
    pb.wt("vector", "st", t)
    pb.op("vector", CP(AF_("mrg", 72, 80, p1=8), pt[0:8, 0:8]), bar=True)
    pb.op("vector", MEMS(AF_("mrg", 80, 88, p1=8), 0.0), bar=True)
    pb.op("vector", SCANOP(AF_("mrg", 88, 96, p1=8), AF_("mrg", 72, 80, p1=8),
                           AF_("mrg", 80, 88, p1=8), 0.0, ALU.add, ALU.add), bar=True)
    pb.op("vector", MEMS(AF_("mrg", 112, 128, p1=8), 0.0), bar=True)
    pb.op("vector", CP(AF_("mrg", 113, 120, p1=8), AF_("mrg", 88, 95, p1=8)))
    tv = pb.op("vector", CP(AF_("mrg", 120, 121, p1=8), AF_("mrg", 95, 96, p1=8)), bar=True)
    pb.wt("tensor", "sv", tv)
    t = pb.op("tensor", TRS(pt[:, :], AF_("mrg", 112, 128, p1=8), cf(ctx, "ident_f"), kp=8))
    pb.wt("vector", "st", t)
    pb.op("vector", CP(AF_("mrg", 128, 136, p1=16), pt[0:16, 0:8]), bar=True)

    for eo in range(E_OWN):
        colap = AF_("mrg", 128 + eo, 128 + eo + 1, p1=16)
        pb.wt("tensor", "sv", pb.tag("sv"))
        t = pb.op("tensor", TRS(pt[:, :], colap, cf(ctx, "ident_f"), kp=16))
        pb.wt("vector", "st", t)
        tcp = pb.op("vector", CP(AF_("mrg", 136, 152, p1=1), pt[0:1, 0:16]), bar=True)
        brow = AF_("mrg", 136, 144, p1=1)
        trow = AF_("mrg", 144, 145, p1=1)
        pb.wt("tensor", "sv", tcp)
        t = pb.op("tensor", MM(P[4][:, 0:8], cf(ctx, "ones_row_f"), brow, True, True))
        t2 = pb.op("tensor", MM(P[4][:, 8:9], cf(ctx, "ones_row_f"), trow, True, True))
        pb.wt("vector", "st", t2)
        bb = AF_("mrg", 152, 160)
        totc = AF_("mrg", 160, 161)
        pb.op("vector", CP(bb, P[4][:, 0:8]))
        tv = pb.op("vector", CP(totc, P[4][:, 8:9]), bar=True)
        pb.wt("tensor", "sv", tv)
        if "dbg" in taps and eo == 0 and app == 0:
            pb.wt("sync", "sv", pb.tag("sv"))
            pb.dma("sync", DMA(tap["dbg"][:, 0:8], bb))
            pb.dma("sync", lambda e, S: e.dma_start(out=tap["dbg"][:, 8:9], in_=totc, allow_slow_non_contiguous=True))
            pb.dma("sync", DMA(tap["dbg"][0:8, 16:24], AF_("mrg", 72, 80, p1=8)))
            pb.dma("sync", DMA(tap["dbg"][0:8, 24:32], AF_("mrg", 88, 96, p1=8)))
            pb.dma("sync", DMA(tap["dbg"][0:16, 32:40], AF_("mrg", 128, 136, p1=16)))
            pb.dma("sync", DMA(tap["dbg"][0:64, 40:48], AF_("mrg", 64, 72, p1=64)))
            _dbgsrc = AF_("mrg", 136, 152, p1=1)
            pb.dma("sync", lambda e, S, _d=_dbgsrc: e.dma_start(out=tap["dbg"][0:1, 48:64], in_=_d, allow_slow_non_contiguous=True))
        for tl in range(8):
            pg = AF_("mrg", 161, 162)
            pb.op("vector", TS(pg, cf(ctx, "iota_col"), float(tl * 128), ALU.add), bar=True)
            ge = AF_("mrg", 162, 170)
            pb.op("vector", TS(ge, bb, pg, ALU.is_le), bar=True)
            gb = AF_("mrg", 170, 178)
            pb.op("vector", TT(gb, ge, bb, ALU.mult), bar=True)
            rof = AF_("mrg", 178, 179)
            bsel = AF_("mrg", 179, 180)
            pb.op("vector", RED(rof, ge, ALU.add), bar=True)
            pb.op("vector", RED(bsel, gb, ALU.max), bar=True)
            pb.op("vector", TS(rof, rof, 1.0, ALU.subtract), bar=True)
            jj = AF_("mrg", 180, 181)
            pb.op("vector", TT(jj, pg, bsel, ALU.subtract), bar=True)
            gx = AF_("mrg", 181, 182)
            pb.op("vector", TS(gx, rof, 65536.0, ALU.mult, float(eo) * 1024.0, ALU.add), bar=True)
            pb.op("vector", TT(gx, gx, cf(ctx, "eg1024_col"), ALU.add), bar=True)
            pb.op("vector", TT(gx, gx, jj, ALU.add), bar=True)
            mval = AF_("mrg", 182, 183)
            pb.op("vector", TS(mval, pg, totc, ALU.is_ge), bar=True)
            pb.op("vector", STT(gx, mval, BIG, gx, ALU.mult, ALU.add), bar=True)
            icol = ibuf[:, 8 + tl:8 + tl + 1]
            tv = pb.op("vector", CP(icol, gx), bar=True)
            pb.wt("gpsimd", "sv", tv)
            tw2 = AF_("mrg", 512 + eo * 16 + tl * 2, 512 + eo * 16 + tl * 2 + 2)
            pb.wt("gpsimd", cc_l, 1)
            tqg = pb.dma("gpsimd", lambda e, S, icol=icol, tw2=tw2: e.indirect_dma_start(
                out=tw2, out_offset=None,
                in_=dr["lall"], in_offset=IndirectOffsetOnAxis(icol, 0),
                bounds_check=pb.breg(e, S, "b_lall", NCORE * E_TOT * CAP - 1),
                oob_is_err=False))
            pb.wtd("vector", "dg", tqg)
            xr = AF_("mrg", 183, 184)
            tkc = AF_("mrg", 512 + eo * 16 + tl * 2, 512 + eo * 16 + tl * 2 + 1)
            pb.op("vector", STT(xr, mval, BIG2, tkc, ALU.mult, ALU.add), bar=True)
            xcol = ibuf[:, 16 + tl:16 + tl + 1]
            pdcol = ibuf[:, 64 + eo * 8 + tl:64 + eo * 8 + tl + 1]
            pb.op("vector", CP(xcol, xr))
            tv = pb.op("vector", CP(pdcol, xr), bar=True)
            pb.wt("gpsimd", "sv", tv)
            gtile = AB("gt", (tl % 2) * 2048, (tl % 2) * 2048 + 2048)
            pb.wt("gpsimd", cc_tag, 1)
            pb.wtd("gpsimd", "dq", pb.tag("dq"))
            tqx = pb.dma("gpsimd", lambda e, S, xcol=xcol, gtile=gtile: e.indirect_dma_start(
                out=gtile, out_offset=None,
                in_=dr["tag_out"], in_offset=IndirectOffsetOnAxis(xcol, 0),
                bounds_check=pb.breg(e, S, "b_tag", NCORE * NTOK - 1),
                oob_is_err=False))
            pb.wtd("sync", "dg", tqx)
            pb.dma("sync", DMA(dr["xdisp"][eo * CAP + tl * 128:eo * CAP + (tl + 1) * 128, :],
                               gtile))
        if app == 0 and eo == 0:
            if "gidx" in taps:
                pb.wt("sync", "sv", pb.tag("sv"))
                pb.dma("sync", DMA(tap["gidx"][:, :], ibuf[:, 8:16]))
            if "xrow" in taps:
                pb.wt("sync", "sv", pb.tag("sv"))
                pb.dma("sync", DMA(tap["xrow"][:, :], ibuf[:, 16:24]))
    if "pdst" in taps:
        pb.wt("sync", "sv", pb.tag("sv"))
        pb.dma("sync", DMA(tap["pdst"][:, :], ibuf[:, 64:128]))

    t_xdisp = pb.tag("dq")

    prev_scatter = pb.tag("dg")
    up_done = 0
    for eo in range(E_OWN):
        pb.wt("sync", "dq", t_xdisp)
        pb.wt("sync", "st", ctx.t_shared_mm)
        if eo > 0:
            pb.wt("sync", "st", up_done)
        for c in range(NCH):
            pb.dma("sync", DMA(AB("tt", c * 1024, (c + 1) * 1024),
                               dr["xdisp"][eo * CAP:(eo + 1) * CAP, c * 128:(c + 1) * 128],
                               transpose=True))
        t_xeT = pb.tag("dq")
        pb.wtd("tensor", "dq", t_xeT)
        for mc in range(4):
            w = AB("wa") if mc % 2 == 0 else AB("wb")
            pb.wt("gpsimd", "st", pb.tag("st"))
            tqw = pb.dma("gpsimd", DMA(
                w, EI["wup"][eo * 128:(eo + 1) * 128, mc * 2048:(mc + 1) * 2048]))
            pb.wtd("tensor", "dg", tqw)
            for nh in (0, 1):
                pp = P[(mc % 2) * 2 + nh]
                for kc in range(16):
                    t = pb.op("tensor", MM(pp[:, 0:512], w[:, kc * 128:(kc + 1) * 128],
                                           AB("tt", kc * 1024 + nh * 512,
                                              kc * 1024 + nh * 512 + 512),
                                           kc == 0, kc == 15),
                              inc=(None if kc < 15 else "auto"))
            t_up = pb.tag("st")
            if mc < 2:
                pb.wt("scalar", "st", t_up)
                for nh in (0, 1):
                    tsc = pb.op("scalar", ACT(AB("gt", mc * 1024 + nh * 512,
                                                 mc * 1024 + nh * 512 + 512),
                                              P[(mc % 2) * 2 + nh][:, 0:512], AF.Silu))
                pb.wt("tensor", "ss", tsc)
            else:
                pb.wt("vector", "st", t_up)
                for nh in (0, 1):
                    tvv = pb.op("vector", CP(AB("gt", mc * 1024 + nh * 512,
                                                mc * 1024 + nh * 512 + 512),
                                             P[(mc % 2) * 2 + nh][:, 0:512]))
                pb.wt("tensor", "sv", tvv)
        t_gv = pb.tag("sv")
        t_gs = pb.tag("ss")
        pb.wt("vector", "ss", t_gs)
        pb.wt("vector", "sv", t_gv)
        for c2 in (0, 1):
            tvh = pb.op("vector", TT(AB("hT", c2 * 1024, (c2 + 1) * 1024),
                                     AB("gt", c2 * 1024, (c2 + 1) * 1024),
                                     AB("gt", 2048 + c2 * 1024, 2048 + (c2 + 1) * 1024),
                                     ALU.mult), bar=True)
        pb.wt("tensor", "sv", tvh)
        up_done = pb.tag("st")
        pb.wt("gpsimd", "st", up_done)
        tqd = pb.dma("gpsimd", DMA(AB("wa"), EI["wdn"][eo * 128:(eo + 1) * 128, 0:2048]))
        tqd2 = pb.dma("gpsimd", DMA(AB("wb"), EI["wdn"][eo * 128:(eo + 1) * 128, 2048:4096]))
        pb.wtd("tensor", "dg", tqd2)
        for rt in range(8):
            pb.wtd("vector", "dg", prev_scatter if rt == 0 else pb.tag("dg"))
            for nh4 in range(4):
                pp = P[4 + nh4 % 2]
                for kc in (0, 1):
                    w = AB("wa") if kc == 0 else AB("wb")
                    t = pb.op("tensor", MM(pp[:, 0:512],
                                           AB("hT", kc * 1024 + rt * 128,
                                              kc * 1024 + (rt + 1) * 128),
                                           w[:, nh4 * 512:(nh4 + 1) * 512], kc == 0, kc == 1),
                              inc=(None if kc == 0 else "auto"))
                pb.wt("vector", "st", t)
                wcolap = AF_("mrg", 512 + eo * 16 + rt * 2 + 1, 512 + eo * 16 + rt * 2 + 2)
                tv = pb.op("vector", TS(AB("drow", nh4 * 512, (nh4 + 1) * 512),
                                        pp[:, 0:512], wcolap, ALU.mult), bar=True)
                pb.wt("tensor", "sv", tv)
            pdcol = ibuf[:, 64 + eo * 8 + rt:64 + eo * 8 + rt + 1]
            pb.wt("gpsimd", "sv", pb.tag("sv"))
            pb.wt("gpsimd", "dg", prev_scatter)
            pb.wtd("gpsimd", "dq", ctx.t_pzero)
            _drow = AB("drow")
            pb.dma("gpsimd", lambda e, S, pdcol=pdcol, _drow=_drow: e.indirect_dma_start(
                out=pacc, out_offset=IndirectOffsetOnAxis(pdcol, 0),
                in_=_drow, in_offset=None,
                bounds_check=pb.breg(e, S, "b_pacc", NCORE * NTOK - 1),
                oob_is_err=False, compute_op=ALU.add))
        prev_scatter = pb.tag("dg")
        if "hrow" in taps and eo == 0:
            pb.wt("sync", "sv", pb.tag("sv"))
            pb.dma("sync", DMA(tap["hrow"][:, :], AB("hT", 0, 16)))
    pb.wt("gpsimd", "dg", prev_scatter)
    pb.cc(lambda e, S: e.collective_compute(
        "ReduceScatter", ALU.add, replica_groups=[list(range(NCORE))],
        ins=[pacc.opt()], outs=[dr[f"rsout{app}"].opt()]), f"cc_rs{app}")


def emit_routed_add(ctx, app):
    pb, dr, xs, AB, AF_ = ctx.pb, ctx.dr, ctx.xs, ctx.AB, ctx.AF
    pb.wt("sync", f"cc_rs{app}", 1)
    for c in range(NCH):
        tile = AB("gt", (c % 2) * 2048, (c % 2) * 2048 + 1024)
        pb.wt("sync", "sv", pb.tag("sv"))
        tq = pb.dma("sync", DMA(tile, dr[f"rsout{app}"][:, c * 128:(c + 1) * 128],
                                transpose=True))
        pb.wtd("vector", "dq", tq)
        xc = xs[:, c * 1024:(c + 1) * 1024]
        f32t = AF_("t1") if c % 2 == 0 else AF_("t2")
        pb.op("vector", CP(f32t, tile), bar=True)
        pb.op("vector", STT(xc, f32t, 0.5, xc, ALU.mult, ALU.add), bar=True)
    tag = pb.tag("sv")
    if app == 0:
        ctx.x1_tag = tag
        if "x1" in ctx.taps:
            pb.wt("sync", "sv", tag)
            for c in range(NCH):
                pb.dma("sync", DMA(ctx.tap_t["x1"][c * 128:(c + 1) * 128, :],
                                   xs[:, c * 1024:(c + 1) * 1024]))
    else:
        ctx.x3_tag = tag



def emit_mamba(ctx):
    pb, EI, dr = ctx.pb, ctx.EI, ctx.dr
    xs, AB, AF_ = ctx.xs, ctx.AB, ctx.AF
    P, pt, ptb = ctx.P, ctx.pt, ctx.ptb
    taps = ctx.taps
    tap = ctx.tap_t

    AB.clear()
    AB.alloc("xn2", 16384, at=0)
    AB.alloc("yg", 16384, at=16384)
    AB.alloc("ebx", 2048, at=16384)
    AB.alloc("wa", 2048, at=32768)
    AB.alloc("wb", 2048, at=34816)
    AB.alloc("xi0", 1056, at=36864)
    AB.alloc("xi1", 1056, at=37920)
    AB.alloc("cm", 5120, at=38976)
    AB.alloc("ldb", 1024, at=44096)
    AB.alloc("xh", 64, at=45120)
    AB.alloc("wh2ib", 4096, at=36864)
    AB.alloc("zwork", 2048, at=40960)
    AF_.clear()
    AF_.alloc("stats", 3072, at=0)
    AF_.alloc("srow", 4096, at=3072)
    AF_.alloc("hal", 512, at=7168)
    AF_.alloc("t1", 1024, at=8192)
    AF_.alloc("t2", 1024, at=9216)
    AF_.alloc("Bf", 2048, at=0)
    AF_.alloc("Cf", 2048, at=2048)
    AF_.alloc("dA", 2048, at=4096)
    AF_.alloc("uf", 2048, at=6144)
    AF_.alloc("sc1", 2048, at=8192)
    AF_.alloc("s1", 1024, at=10240)
    AF_.alloc("s2", 1024, at=4096)
    AF_.alloc("s3", 1024, at=11264)
    AF_.alloc("halall", 2048, at=10240)
    AF_.alloc("srow2", 128, at=12288)

    x1t = ctx.x1_tag
    pb.wt("tensor", "sv", x1t)
    for c in range(NCH):
        t = pb.op("tensor", TRS(pt[:, :], xs[:, c * 1024 + 1020:c * 1024 + 1024],
                                cf(ctx, "ident_f")))
        pb.wt("vector", "st", t)
        tv = pb.op("vector", CP(AF_("Bf", c * 128, (c + 1) * 128, p1=4),
                                pt[0:4, 0:128]), bar=True)
        pb.wt("tensor", "sv", tv)
    t_hext = pb.tag("sv")
    pb.wt("sync", "sv", t_hext)
    th = pb.dma("sync", DMA(dr["halo_in"][:, :], AF_("Bf", 0, 2048, p1=4)))
    pb.wtd("vector", "dq", th)
    pb.wtd("gpsimd", "dq", th)
    pb.cc(lambda e, S: e.collective_compute(
        "AllGather", ALU.bypass, replica_groups=[list(range(NCORE))],
        ins=[dr["halo_in"].opt()], outs=[dr["halo_all"].opt()]), "cc_halo")

    emit_ln(ctx, "xn2", wait_tags=[("sv", x1t)], rms_fuse=True)
    mean_b = AF_("stats", 0, 1024)
    rstd_b = AF_("stats", 1024, 2048)
    rms_b = AF_("stats", 2048, 3072)
    for c in range(NCH):
        xc = xs[:, c * 1024:(c + 1) * 1024]
        tc = AF_("t1") if c % 2 == 0 else AF_("t2")
        pb.op("vector", STT(tc, xc, 1.0, mean_b, ALU.mult, ALU.subtract), bar=True)
        pb.op("vector", TT(tc, tc, rstd_b, ALU.mult), bar=True)
        pb.op("vector", TT(xc, xc, tc, ALU.add), bar=True)
        pb.op("vector", TT(tc, tc, rms_b, ALU.mult), bar=True)
        wcol = ctx.cbf[0:128, ctx.CF["wnormc"][0] + c:ctx.CF["wnormc"][0] + c + 1]
        tv = pb.op("vector", TS(AB("xn2", c * 1024, (c + 1) * 1024), tc, wcol, ALU.mult),
                   bar=True)
    t_xn2 = pb.tag("sv")
    ctx.x15_tag = t_xn2
    if "xn2" in taps:
        pb.wt("sync", "sv", t_xn2)
        pb.dma("sync", DMA(tap["xn2"][:, :], AB("xn2")))

    pb.wt("sync", "cc_halo", 1)
    th2 = pb.dma("sync", DMA(AF_("halall", 0, 2048, p1=32), dr["halo_all"][:, :]))
    pb.wtd("tensor", "dq", th2)
    t = None
    for h4 in range(4):
        t = pb.op("tensor", MM(P[0][0:4, 0:512], cf(ctx, "halo_selm"),
                               AF_("halall", h4 * 512, (h4 + 1) * 512, p1=32),
                               True, True))
        pb.wt("vector", "st", t)
        tv = pb.op("vector", CP(AF_("uf", h4 * 512, (h4 + 1) * 512, p1=4), P[0][0:4, 0:512]),
                   bar=True)
        pb.wt("tensor", "sv", tv)
    for c in range(NCH):
        t = pb.op("tensor", TRS(pt[:, :], AF_("uf", c * 128, (c + 1) * 128, p1=4),
                                cf(ctx, "ident_f"), kp=4))
        pb.wt("vector", "st", t)
        tv = pb.op("vector", CP(AF_("hal", c * 4, (c + 1) * 4), pt[:, 0:4]), bar=True)
        pb.wt("tensor", "sv", tv)
        pb.wt("scalar", "sv", tv)
    for c in range(NCH):
        tsq = pb.op("scalar", ACT(AF_("hal", 64 + c * 4, 64 + (c + 1) * 4),
                                  AF_("hal", c * 4, (c + 1) * 4), AF.Square))
        pb.wt("tensor", "ss", tsq)
        pb.op("tensor", MM(pt[0:1, 0:4], cf(ctx, "ones_col_f"),
                           AF_("hal", c * 4, (c + 1) * 4), c == 0, c == NCH - 1),
              inc=(None if c < NCH - 1 else "auto"))
        t = pb.op("tensor", MM(P[1][0:1, 0:4], cf(ctx, "ones_col_f"),
                               AF_("hal", 64 + c * 4, 64 + (c + 1) * 4), c == 0, c == NCH - 1))
    pb.wt("vector", "st", t)
    hr = lambda lo, hi: AF_("hal", lo, hi, p1=1)
    pb.op("vector", CP(hr(128, 132), pt[0:1, 0:4]))
    pb.op("vector", CP(hr(132, 136), P[1][0:1, 0:4]), bar=True)
    pb.op("vector", TS(hr(136, 140), hr(128, 132), 1.0 / D, ALU.mult), bar=True)
    pb.op("vector", TT(hr(144, 148), hr(136, 140), hr(136, 140), ALU.mult), bar=True)
    pb.op("vector", STT(hr(144, 148), hr(132, 136), 1.0 / D, hr(144, 148),
                        ALU.mult, ALU.subtract), bar=True)
    pb.wt("scalar", "sv", pb.tag("sv"))
    t = pb.op("scalar", ACT(hr(148, 152), hr(144, 148), AF.Identity,
                            bias=cf(ctx, "eps_ln")[0:1, :]))
    pb.wt("vector", "ss", t)
    pb.op("vector", RECIP(hr(148, 152), hr(148, 152)), bar=True)
    pb.op("vector", TT(hr(148, 152), hr(144, 148), hr(148, 152), ALU.mult), bar=True)
    pb.wt("scalar", "sv", pb.tag("sv"))
    t = pb.op("scalar", ACT(hr(148, 152), hr(148, 152), AF.Sqrt,
                            bias=cf(ctx, "eps_rms")[0:1, :]))
    pb.wt("vector", "ss", t)
    pb.op("vector", RECIP(hr(148, 152), hr(148, 152)), bar=True)
    pb.wt("scalar", "sv", pb.tag("sv"))
    t = pb.op("scalar", ACT(hr(144, 148), hr(144, 148), AF.Sqrt,
                            bias=cf(ctx, "eps_ln")[0:1, :]))
    pb.wt("vector", "ss", t)
    pb.op("vector", RECIP(hr(144, 148), hr(144, 148)), bar=True)
    pb.op("vector", TT(hr(144, 148), hr(144, 148), hr(148, 152), ALU.mult), bar=True)
    t_hrow = pb.tag("sv")
    pb.wt("tensor", "sv", t_hrow)
    t = pb.op("tensor", MM(P[0][:, 0:4], cf(ctx, "ones_row_f"), hr(136, 140), True, True),
              inc=None)
    t = pb.op("tensor", MM(P[0][:, 4:8], cf(ctx, "ones_row_f"), hr(144, 148), True, True))
    pb.wt("vector", "st", t)
    pb.op("vector", CP(AF_("hal", 152, 156), P[0][:, 0:4]))
    pb.op("vector", CP(AF_("hal", 156, 160), P[0][:, 4:8]), bar=True)
    for c in range(NCH):
        hx = AF_("hal", c * 4, (c + 1) * 4)
        tmp4 = AF_("hal", 160, 164)
        pb.op("vector", TT(tmp4, hx, AF_("hal", 152, 156), ALU.subtract), bar=True)
        pb.op("vector", TT(tmp4, tmp4, AF_("hal", 156, 160), ALU.mult), bar=True)
        wcol = ctx.cbf[0:128, ctx.CF["wnormc"][0] + c:ctx.CF["wnormc"][0] + c + 1]
        tv = pb.op("vector", TS(AB("xh", c * 4, (c + 1) * 4), tmp4, wcol, ALU.mult), bar=True)
    t_xh = pb.tag("sv")

    def stream_w(mc, buf):
        return pb.dma("gpsimd", DMA(AB(buf), EI["w_inT"][:, mc * 2048:(mc + 1) * 2048]))

    pb.wt("tensor", "sv", t_xn2)
    pb.wt("tensor", "sv", t_xh)

    def proj_mm(mc, buf, pdst0, pdst1, with_halo=False, hal_psum=None):
        w = AB(buf)
        for kc in range(16):
            pb.op("tensor", MM(pdst0, w[:, kc * 128:(kc + 1) * 128],
                               AB("xn2", kc * 1024, kc * 1024 + 512), kc == 0, kc == 15),
                  inc=(None if kc < 15 else "auto"))
            t = pb.op("tensor", MM(pdst1, w[:, kc * 128:(kc + 1) * 128],
                                   AB("xn2", kc * 1024 + 512, kc * 1024 + 1024),
                                   kc == 0, kc == 15),
                      inc=(None if kc < 15 else "auto"))
            if with_halo:
                t = pb.op("tensor", MM(hal_psum, w[:, kc * 128:(kc + 1) * 128],
                                       AB("xh", kc * 4 + 1, kc * 4 + 4), kc == 0, kc == 15),
                          inc=(None if kc < 15 else "auto"))
        return t

    tw = stream_w(64, "wa")
    pb.wtd("tensor", "dg", tw)
    t = proj_mm(64, "wa", P[0][:, 0:512], P[1][:, 0:512])
    pb.wt("vector", "st", t)
    pb.op("vector", CP(AF_("s1", 0, 512, p1=16), P[0][0:16, 0:512]))
    pb.op("vector", CP(AF_("s1", 512, 1024, p1=16), P[1][0:16, 0:512]), bar=True)
    t_dtraw = pb.tag("sv")
    dsts = [("Bf", 0), ("Bf", 1024), ("Cf", 0), ("Cf", 1024)]
    for j, mc in enumerate((65, 66, 67, 68)):
        buf = "wb" if j % 2 == 0 else "wa"
        pb.wt("gpsimd", "st", pb.tag("st"))
        tw = stream_w(mc, buf)
        pb.wtd("tensor", "dg", tw)
        t = proj_mm(mc, buf, P[2][:, 0:512], P[3][:, 0:512])
        pb.wt("vector", "st", t)
        nm, off = dsts[j]
        pb.op("vector", CP(AF_(nm, off, off + 512), P[2][:, 0:512]))
        tv = pb.op("vector", CP(AF_(nm, off + 512, off + 1024), P[3][:, 0:512]), bar=True)
        pb.wt("tensor", "sv", tv)
    for ci in range(32):
        mc = 32 + ci
        buf = "wa" if ci % 2 == 0 else "wb"
        pb.wt("gpsimd", "st", pb.tag("st"))
        tw = stream_w(mc, buf)
        pb.wtd("tensor", "dg", tw)
        t = proj_mm(mc, buf, P[ci % 2][:, 0:512], P[2 + ci % 2][:, 0:512],
                    with_halo=True, hal_psum=pt[:, 0:3])
        xi = AB("xi0") if ci % 2 == 0 else AB("xi1")
        xi_t = ctx.AB.t
        a0 = ctx.AB.map["xi0" if ci % 2 == 0 else "xi1"][0]
        pb.wt("vector", "st", t)
        pb.op("vector", CP(xi_t[:, a0:a0 + 3], pt[:, 0:3]))
        pb.op("vector", CP(xi_t[:, a0 + 3:a0 + 515], P[ci % 2][:, 0:512]))
        tv = pb.op("vector", CP(xi_t[:, a0 + 515:a0 + 1027], P[2 + ci % 2][:, 0:512]),
                   bar=True)
        pb.wt("tensor", "sv", tv)
        wc0 = ctx.CF["wconv"][0]
        for k in range(4):
            wcol = ctx.cbf[0:128, wc0 + ci * 4 + k:wc0 + ci * 4 + k + 1]
            pb.op("vector", TS(AB("cm", k * 1024, (k + 1) * 1024),
                               xi_t[:, a0 + k:a0 + k + 1024], wcol, ALU.mult))
        pb.wt("vector", "sv", pb.tag("sv"))
        pb.op("vector", TT(AB("cm", 0, 1024), AB("cm", 0, 1024), AB("cm", 1024, 2048),
                           ALU.add))
        pb.op("vector", TT(AB("cm", 2048, 3072), AB("cm", 2048, 3072), AB("cm", 3072, 4096),
                           ALU.add), bar=True)
        tv = pb.op("vector", TT(AB("cm", 0, 1024), AB("cm", 0, 1024), AB("cm", 2048, 3072),
                                ALU.add), bar=True)
        pb.wt("scalar", "sv", tv)
        tsc = pb.op("scalar", ACT(AB("cm", 4096, 5120), AB("cm", 0, 1024), AF.Silu))
        pb.wt("tensor", "ss", tsc)
        for h in (0, 1):
            t = pb.op("tensor", MM(P[4 + h][0:16, 0:512],
                                   cb(ctx, "xmean_sel_b")[:, ci * 16:(ci + 1) * 16],
                                   AB("cm", 4096 + h * 512, 4096 + h * 512 + 512),
                                   ci == 0, ci == 31),
                      inc=(None if h == 0 else "auto"))
        t_cm = pb.tag("st")
        pb.wt("vector", "st", t_cm)
    t_xm = pb.tag("st")
    pb.wt("vector", "st", t_xm)
    pb.op("vector", CP(AF_("s2", 0, 512, p1=16), P[4][0:16, 0:512]))
    pb.op("vector", CP(AF_("s2", 512, 1024, p1=16), P[5][0:16, 0:512]), bar=True)
    if "xbar" in taps:
        pb.wt("sync", "sv", pb.tag("sv"))
        pb.dma("sync", DMA(tap["xbar"][:, :], AF_("s2", 0, 1024, p1=16)))

    s1 = lambda lo, hi: AF_("s1", lo, hi, p1=16)
    s2 = lambda lo, hi: AF_("s2", lo, hi, p1=16)
    s3 = lambda lo, hi: AF_("s3", lo, hi, p1=16)
    dtbc = ctx.cbf[0:16, ctx.CF["dtb"][0]:ctx.CF["dtb"][0] + 1]
    pb.wt("scalar", "sv", pb.tag("sv"))
    ta = pb.op("scalar", ACT(s3(0, 1024), s1(0, 1024), AF.Abs, bias=dtbc))
    pb.wt("scalar", "ss", ta)
    ta = pb.op("scalar", ACT(s3(0, 1024), s3(0, 1024), AF.Exp, scale=-1.0))
    pb.wt("scalar", "ss", ta)
    ta = pb.op("scalar", ACT(s3(0, 1024), s3(0, 1024), AF.Ln,
                             bias=cf(ctx, "one_colf")[0:16, :]))
    pb.wt("scalar", "ss", ta)
    ta = pb.op("scalar", ACT(s1(0, 1024), s1(0, 1024), AF.Relu, bias=dtbc))
    pb.wt("vector", "ss", ta)
    pb.op("vector", TT(s1(0, 1024), s1(0, 1024), s3(0, 1024), ALU.add), bar=True)
    if "dtsp" in taps:
        pb.wt("sync", "sv", pb.tag("sv"))
        pb.dma("sync", DMA(tap["dtsp"][:, :], s1(0, 1024)))
    acol = AF_("srow2", 0, 1, p1=16)
    alogc = ctx.cbf[0:16, ctx.CF["alog"][0]:ctx.CF["alog"][0] + 1]
    ta = pb.op("scalar", ACT(acol, alogc, AF.Exp))
    pb.wt("vector", "ss", ta)
    pb.op("vector", TS(acol, acol, -1.0, ALU.mult), bar=True)
    pb.op("vector", TT(s3(0, 1024), s1(0, 1024), s2(0, 1024), ALU.mult), bar=True)
    pb.op("vector", TS(s1(0, 1024), s1(0, 1024), acol, ALU.mult), bar=True)
    pb.op("vector", RED(AF_("srow2", 1, 2, p1=16), s1(0, 1024), ALU.add), bar=True)
    tv = pb.op("vector", CP(AB("ldb", 0, 1024, p1=16), s1(0, 1024)), bar=True)
    tv2 = pb.op("vector", CP(AB("cm", 0, 1024, p1=16), s3(0, 1024)), bar=True)
    pb.wt("tensor", "sv", tv2)
    pb.wtd("scalar", "dq")
    pb.wtd("vector", "dq")
    for ci in (0, 1):
        hb = cb(ctx, "headbc_b")[:, ci * 128:(ci + 1) * 128]
        for h in (0, 1):
            t = pb.op("tensor", MM(P[0][:, 0:512], hb,
                                   AB("ldb", h * 512, h * 512 + 512, p1=16), True, True))
            t2 = pb.op("tensor", MM(P[1][:, 0:512], hb,
                                    AB("cm", h * 512, h * 512 + 512, p1=16), True, True))
            pb.wt("scalar", "st", t)
            tsc = pb.op("scalar", ACT(AF_("dA", ci * 1024 + h * 512, ci * 1024 + h * 512 + 512),
                                      P[0][:, 0:512], AF.Exp))
            pb.wt("vector", "st", t2)
            pb.op("vector", CP(AF_("uf", ci * 1024 + h * 512, ci * 1024 + h * 512 + 512),
                               P[1][:, 0:512]), bar=True)
            pb.wt("tensor", "sv", pb.tag("sv"))
            pb.wt("tensor", "ss", tsc)
        pb.op("vector", TT(AF_("uf", ci * 1024, (ci + 1) * 1024),
                           AF_("uf", ci * 1024, (ci + 1) * 1024),
                           AF_("Bf", ci * 1024, (ci + 1) * 1024), ALU.mult), bar=True)
    t_scanin = pb.tag("sv")
    pb.wt("vector", "ss", pb.tag("ss"))
    if "dA2" in taps:
        pb.wt("sync", "sv", pb.tag("sv"))
        pb.wt("sync", "ss", pb.tag("ss"))
        pb.dma("sync", DMA(tap["dA2"][:, :], AF_("dA")))
        pb.dma("sync", DMA(tap["uf2"][:, :], AF_("uf")))
    for ci in (0, 1):
        pb.op("vector", SCANOP(AF_("sc1", ci * 1024, (ci + 1) * 1024),
                               AF_("dA", ci * 1024, (ci + 1) * 1024),
                               AF_("uf", ci * 1024, (ci + 1) * 1024), 0.0), bar=True)
    pb.wt("sync", "sv", pb.tag("sv"))
    pb.dma("sync", DMA(dr["sum_in"][0:16, :], AF_("srow2", 1, 2, p1=16)))
    pb.dma("sync", DMA(dr["sum_in"][16:144, :], AF_("sc1", 1023, 1024)))
    tq = pb.dma("sync", DMA(dr["sum_in"][144:272, :], AF_("sc1", 2047, 2048)))
    pb.wtd("gpsimd", "dq", tq)
    pb.cc(lambda e, S: e.collective_compute(
        "AllGather", ALU.bypass, replica_groups=[list(range(NCORE))],
        ins=[dr["sum_in"].opt()], outs=[dr["sum_all"].opt()]), "cc_sum")
    pb.wt("sync", "cc_sum", 1)
    for r in range(NCORE):
        pb.dma("sync", DMA(AF_("srow2", 8 + r, 9 + r, p1=16),
                           dr["sum_all"][r * 272:r * 272 + 16, :]))
        pb.dma("sync", DMA(AF_("srow2", 16 + r, 17 + r, p1=128),
                           dr["sum_all"][r * 272 + 16:r * 272 + 144, :]))
        tq = pb.dma("sync", DMA(AF_("srow2", 24 + r, 25 + r, p1=128),
                                dr["sum_all"][r * 272 + 144:r * 272 + 272, :]))
    pb.wtd("tensor", "dq", tq)
    t = pb.op("tensor", TRS(pt[:, :], AF_("srow2", 8, 16, p1=16), cf(ctx, "ident_f"), kp=16))
    pb.wt("vector", "st", t)
    pb.op("vector", CP(AF_("srow2", 32, 48, p1=8), pt[0:8, 0:16]), bar=True)
    pb.wt("tensor", "sv", pb.tag("sv"))
    t = pb.op("tensor", MM(P[0][0:8, 0:16], cf(ctx, "stitch_w"), AF_("srow2", 32, 48, p1=8),
                           True, True))
    pb.wt("scalar", "st", t)
    t = pb.op("scalar", ACT(AF_("srow2", 48, 64, p1=8), P[0][0:8, 0:16], AF.Exp))
    pb.wt("vector", "ss", t)
    pb.op("vector", TS(AF_("srow2", 48, 64, p1=8), AF_("srow2", 48, 64, p1=8),
                       cf(ctx, "stitch_pick"), ALU.mult), bar=True)
    pb.wt("tensor", "sv", pb.tag("sv"))
    t = pb.op("tensor", TRS(pt[:, :], AF_("srow2", 48, 64, p1=8), cf(ctx, "ident_f"), kp=8))
    pb.wt("vector", "st", t)
    pb.op("vector", CP(AF_("srow2", 64, 72, p1=16), pt[0:16, 0:8]), bar=True)
    pb.op("vector", CP(AB("ldb", 0, 8, p1=16), AF_("srow2", 64, 72, p1=16)), bar=True)
    pb.wt("tensor", "sv", pb.tag("sv"))
    for ci in (0, 1):
        hb = cb(ctx, "headbc_b")[:, ci * 128:(ci + 1) * 128]
        t = pb.op("tensor", MM(P[0][:, 0:8], hb, AB("ldb", 0, 8, p1=16), True, True))
        pb.wt("vector", "st", t)
        pb.op("vector", CP(AF_("srow2", 80, 88), P[0][:, 0:8]), bar=True)
        pb.op("vector", TT(AF_("srow2", 80, 88), AF_("srow2", 80, 88),
                           AF_("srow2", (16 if ci == 0 else 24), (24 if ci == 0 else 32)),
                           ALU.mult), bar=True)
        pb.op("vector", RED(AF_("srow2", 88 + ci, 89 + ci), AF_("srow2", 80, 88), ALU.add),
              bar=True)
        pb.wt("tensor", "sv", pb.tag("sv"))
    for ci in (0, 1):
        pb.op("vector", SCANOP(AF_("sc1", ci * 1024, (ci + 1) * 1024),
                               AF_("dA", ci * 1024, (ci + 1) * 1024),
                               AF_("uf", ci * 1024, (ci + 1) * 1024),
                               AF_("srow2", 88 + ci, 89 + ci)), bar=True)
    if "sc2" in taps:
        pb.wt("sync", "sv", pb.tag("sv"))
        pb.dma("sync", DMA(tap["sc2"][:, :], AF_("sc1")))
    for ci in (0, 1):
        pb.op("vector", TT(AF_("uf", ci * 1024, (ci + 1) * 1024),
                           AF_("sc1", ci * 1024, (ci + 1) * 1024),
                           AF_("Cf", ci * 1024, (ci + 1) * 1024), ALU.mult), bar=True)
    t_yp = pb.tag("sv")
    pb.wt("tensor", "sv", t_yp)
    for h in (0, 1):
        for ci in (0, 1):
            t = pb.op("tensor", MM(P[2][0:16, 0:512],
                                   cf(ctx, "ysum_f")[:, ci * 16:(ci + 1) * 16],
                                   AF_("uf", ci * 1024 + h * 512, ci * 1024 + h * 512 + 512),
                                   ci == 0, ci == 1))
        pb.wt("vector", "st", t)
        pb.op("vector", CP(s3(h * 512, h * 512 + 512), P[2][0:16, 0:512]), bar=True)
        pb.wt("tensor", "sv", pb.tag("sv"))
    if "ypr" in taps:
        pb.wt("sync", "sv", pb.tag("sv"))
        pb.dma("sync", DMA(tap["ypr"][:, :], AF_("uf")))
    if "yscan" in taps:
        pb.wt("sync", "sv", pb.tag("sv"))
        pb.dma("sync", DMA(tap["yscan"][:, :], s3(0, 1024)))
    tyb = pb.op("vector", CP(AB("ldb", 0, 1024, p1=16), s3(0, 1024)), bar=True)

    twh = pb.dma("gpsimd", DMA(AB("wh2ib", 0, 4096, p1=16), EI["wh2i"][:, :]))
    pb.wtd("tensor", "dg", twh)
    pb.wt("tensor", "sv", tyb)
    for half in (0, 1):
        for mc in range(32):
            buf = "wa" if mc % 2 == 0 else "wb"
            pb.wt("gpsimd", "st", pb.tag("st"))
            tw = stream_w(mc, buf)
            pb.wtd("tensor", "dg", tw)
            w = AB(buf)
            for kc in range(16):
                pb.op("tensor", MM(P[0][:, 0:512], w[:, kc * 128:(kc + 1) * 128],
                                   AB("xn2", kc * 1024 + half * 512,
                                      kc * 1024 + half * 512 + 512),
                                   kc == 0, kc == 15),
                      inc=(None if kc < 15 else "auto"))
            t = pb.op("tensor", MM(P[1][:, 0:512], AB("wh2ib", mc * 128, (mc + 1) * 128, p1=16),
                                   AB("ldb", half * 512, half * 512 + 512, p1=16),
                                   True, True))
            pb.wt("scalar", "st", t)
            tsc = pb.op("scalar", ACT(AB("zwork", 0, 512), P[0][:, 0:512], AF.Silu))
            pb.wt("vector", "st", t)
            pb.op("vector", CP(AB("zwork", 512, 1024), P[1][:, 0:512]))
            pb.wt("vector", "ss", tsc)
            pb.wt("vector", "sv", pb.tag("sv"))
            tv = pb.op("vector", TT(AB("yg", mc * 512, (mc + 1) * 512),
                                    AB("zwork", 0, 512), AB("zwork", 512, 1024), ALU.mult),
                       bar=True)
            pb.wt("tensor", "sv", tv)
        t_yg = pb.tag("sv")
        pb.wt("tensor", "sv", t_yg)
        for m in range(16):
            pb.wt("gpsimd", "st", pb.tag("st"))
            tw1 = pb.dma("gpsimd", DMA(AB("wa"), EI["w_outT"][:, m * 4096:m * 4096 + 2048]))
            tw2 = pb.dma("gpsimd", DMA(AB("wb"), EI["w_outT"][:, m * 4096 + 2048:(m + 1) * 4096]))
            pb.wtd("tensor", "dg", tw2)
            for kc in range(32):
                w = AB("wa") if kc < 16 else AB("wb")
                kk = kc % 16
                t = pb.op("tensor", MM(P[2][:, 0:512], w[:, kk * 128:(kk + 1) * 128],
                                       AB("yg", kc * 512, (kc + 1) * 512), kc == 0, kc == 31),
                          inc=(None if kc < 31 else "auto"))
            pb.wt("vector", "st", t)
            xc = xs[:, m * 1024 + half * 512:m * 1024 + half * 512 + 512]
            tv = pb.op("vector", TT(xc, xc, P[2][:, 0:512], ALU.add), bar=True)
            pb.wt("tensor", "sv", tv)
    ctx.x2_tag = pb.tag("sv")
    if "x2" in taps:
        pb.wt("sync", "sv", ctx.x2_tag)
        for c in range(NCH):
            pb.dma("sync", DMA(tap["x2"][c * 128:(c + 1) * 128, :],
                               xs[:, c * 1024:(c + 1) * 1024]))


def emit_output(ctx):
    pb, xs, AF_ = ctx.pb, ctx.xs, ctx.AF
    pt = ctx.pt
    src_tag = ctx.x3_tag if ctx.phases >= 3 else ctx.x1_tag
    pb.wt("tensor", "sv", src_tag)
    AF_.alloc("otile", 2048, at=0) if "otile" not in AF_.map else None
    for rt in range(8):
        for c in range(NCH):
            t = pb.op("tensor", TRS(pt[:, :], xs[:, c * 1024 + rt * 128:c * 1024 + (rt + 1) * 128],
                                    cf(ctx, "ident_f")))
            pb.wt("vector", "st", t)
            tv = pb.op("vector", CP(AF_("otile", c * 128, (c + 1) * 128), pt[:, :]), bar=True)
            pb.wt("tensor", "sv", tv)
        pb.wt("sync", "sv", pb.tag("sv"))
        pb.dma("sync", DMA(ctx.out_t[rt * 128:(rt + 1) * 128, :], AF_("otile")))
        if rt < 7:
            pb.wtd("vector", "dq", pb.tag("dq"))
    ctx.out_tag = pb.tag("dq")


def emit_final_wait(ctx):
    pb = ctx.pb
    for s in ("dq", "dg", "dsc"):
        pb.wt("sync", s, pb.tag(s))
    for s in list(pb.cnt):
        if s.startswith("cc_") and pb.cnt[s]:
            pb.wt("sync", s, 1)



def _prep_lhsT(w):
    K, M = w.shape
    kc, mc = K // 128, M // 128
    return np.ascontiguousarray(
        w.reshape(kc, 128, mc, 128).transpose(1, 2, 0, 3).reshape(128, mc * kc * 128),
        dtype=np.float32)


def make_inputs(inp):
    x = np.asarray(inp["L0_attn_x"], np.float32).reshape(NCORE * NTOK, D)
    wg_ = np.asarray(inp["L0_ffn_w_gate"], np.float32)
    wsu_ = np.asarray(inp["L0_ffn_w_shared_up"], np.float32)
    wsd_ = np.asarray(inp["L0_ffn_w_shared_down"], np.float32)
    wup_ = np.asarray(inp["L0_ffn_w_up"], np.float32)
    wdn_ = np.asarray(inp["L0_ffn_w_down"], np.float32)
    win_ = np.asarray(inp["L0_attn_W_in"], np.float32)
    wout_ = np.asarray(inp["L0_attn_W_out"], np.float32)
    wh2i_ = np.asarray(inp["L0_attn_W_h2i"], np.float32)
    wconv_ = np.asarray(inp["L0_attn_w_conv"], np.float32).reshape(4096, 4)
    wnorm_ = np.asarray(inp["L0_attn_w_norm"], np.float32)
    dtb_ = np.asarray(inp["L0_attn_dt_bias"], np.float32)
    alog_ = np.asarray(inp["L0_attn_A_log"], np.float32)

    wg_l = np.ascontiguousarray(wg_.reshape(16, 128, 64).transpose(1, 0, 2).reshape(128, 1024))
    wsu_l = _prep_lhsT(wsu_)
    wsd_l = _prep_lhsT(wsd_)
    WT = win_.T
    WTa = np.concatenate([WT[:, :8192], WT[:, 8192:8208],
                          np.zeros((2048, 112), np.float32),
                          WT[:, 8208:8464], WT[:, 8464:8720]], axis=1)
    win_l = _prep_lhsT(WTa)
    wout_l = _prep_lhsT(np.ascontiguousarray(wout_.T))
    wh2i_l = np.ascontiguousarray(wh2i_.T)
    wconv_l = np.ascontiguousarray(wconv_.reshape(32, 128, 4).transpose(1, 0, 2).reshape(128, 128))
    wnorm_l = np.ascontiguousarray(wnorm_.reshape(16, 128).T)

    consts0 = {
        "ident_f": np.eye(128, dtype=np.float32),
        "ident_b": np.eye(128, dtype=np.float32).astype(BF),
        "ones_col_b": np.full((128, 1), 1.0, BF),
        "ones_row_b": np.full((1, 128), 1.0, BF),
        "ones_row_f": np.full((1, 128), 1.0, np.float32),
        "iota_col": np.arange(128, dtype=np.float32).reshape(128, 1),
        "iota64": np.broadcast_to(np.arange(64, dtype=np.float32), (128, 64)).copy(),
        "ones_col_f": np.full((128, 1), 1.0, np.float32),
        "eps_ln": np.full((128, 1), LN_EPS, np.float32),
        "eps_rms": np.full((128, 1), RMS_EPS, np.float32),
        "one_colf": np.full((128, 1), 1.0, np.float32),
        "tri_excl_b": (np.arange(128)[:, None] < np.arange(128)[None, :]).astype(BF),
    }
    xm = np.zeros((32, 128, 16), np.float32)
    for ci in range(32):
        xm[ci, :, ci // 2] = 1.0 / 256.0
    consts0["xmean_sel_b"] = np.ascontiguousarray(
        xm.transpose(1, 0, 2).reshape(128, 512)).astype(BF)
    hbm_ = np.zeros((2, 16, 128), np.float32)
    for ci2 in range(2):
        for p_ in range(128):
            hbm_[ci2, ci2 * 8 + p_ // 16, p_] = 1.0
    consts0["headbc_b"] = np.ascontiguousarray(
        hbm_.transpose(1, 0, 2).reshape(16, 256)).astype(BF)
    ys = np.zeros((2, 128, 16), np.float32)
    for ci2 in range(2):
        for p_ in range(128):
            ys[ci2, p_, ci2 * 8 + p_ // 16] = 1.0
    consts0["ysum_f"] = np.ascontiguousarray(ys.transpose(1, 0, 2).reshape(128, 32))

    in_maps = []
    for core in range(NCORE):
        m = {}
        m["x_fm"] = np.ascontiguousarray(x[core * NTOK:(core + 1) * NTOK].T)
        m["wg"] = wg_l
        m["wsu"] = wsu_l
        m["wsd"] = wsd_l
        up8 = np.stack([_prep_lhsT(wup_[core * 8 + i]) for i in range(8)])
        m["wup"] = np.ascontiguousarray(up8.reshape(8 * 128, 8192))
        dn8 = np.stack([np.ascontiguousarray(
            wdn_[core * 8 + i].reshape(2, 128, 2048).transpose(1, 0, 2).reshape(128, 4096))
            for i in range(8)])
        m["wdn"] = np.ascontiguousarray(dn8.reshape(8 * 128, 4096))
        m["w_inT"] = win_l
        m["w_outT"] = wout_l
        m["wh2i"] = wh2i_l
        m["wconv"] = wconv_l
        m["wnormc"] = wnorm_l
        m["dtb"] = dtb_.reshape(16, 1)
        m["alog"] = alog_.reshape(16, 1)
        m.update(consts0)
        sel = np.zeros((64, 8), np.float32)
        for i in range(8):
            sel[core * 8 + i, i] = 1.0
        m["sel_own_f"] = sel
        m["tok_base"] = np.full((128, 1), core * NTOK, np.float32)
        m["eg1024_col"] = np.full((128, 1), core * 8 * 1024.0, np.float32)
        suf = np.zeros((8, 8), np.float32)
        for j in range(8):
            for mm_ in range(8):
                if j < mm_ < core:
                    suf[j, mm_] = 1.0
        m["stitch_w"] = suf
        m["stitch_pick"] = (np.arange(8) < core).astype(np.float32).reshape(8, 1)
        hs = np.zeros((32, 4), np.float32)
        if core > 0:
            hs[(core - 1) * 4:(core - 1) * 4 + 4] = np.eye(4, dtype=np.float32)
        m["halo_selm"] = hs
        in_maps.append(m)
    return in_maps


_prog_cache = {}


def kernel(**inputs):
    taps = tuple(sorted(inputs.pop("_taps", ())))
    phases = inputs.pop("_phases", 99)
    key = (taps, phases)
    if key not in _prog_cache:
        _prog_cache[key] = build_program(taps=taps, phases=phases)
    nc = _prog_cache[key]
    in_maps = make_inputs(inputs)
    res = run_bass_kernel_spmd(nc, in_maps, core_ids=list(range(NCORE)))
    outs = [r["out_t"] for r in res.results]
    full = np.concatenate(outs, axis=0).reshape(1, NCORE * NTOK, D)
    if taps:
        tapout = {t: [r["tap_" + t] for r in res.results] for t in taps}
        return full, tapout
    return full


# version 02
# speedup vs baseline: 1.4117x; 1.4117x over previous
import numpy as np
import ml_dtypes
import concourse.bass as bass
import concourse.mybir as mybir
from concourse.bass_utils import run_bass_kernel_spmd
from concourse.bass_core_rust import IndirectOffsetOnAxis

BF16 = mybir.dt.bfloat16
F32 = mybir.dt.float32
I32 = mybir.dt.int32
ALU = mybir.AluOpType
AF = mybir.ActivationFunctionType
AXX = mybir.AxisListType.X
BF = ml_dtypes.bfloat16

NCORE = 8
NTOK = 1024
D = 2048
NCH = 16
E_TOT = 64
E_OWN = 8
CAP = 1024
TOPK = 6
MT = 8832
NMC = MT // 128
LN_EPS = 1e-5
RMS_EPS = 1.1920929e-07
BIG = 1.0e7
BIG2 = 16384.0

ENGINES = ("sync", "vector", "scalar", "tensor", "gpsimd")
ESEM = {"sync": "sq", "vector": "sv", "scalar": "ss", "tensor": "st", "gpsimd": "sg"}


class PB:
    def __init__(self):
        self.ops = {e: [] for e in ENGINES}
        self.semnames = []
        self.cnt = {}
        self.bregs = {}

    def sem(self, name):
        self.semnames.append(name)
        self.cnt[name] = 0

    def wt(self, eng, sem, val):
        if sem in ("dq", "dg", "dsc"):
            val = self.cnt[sem]
        if val and val > 0:
            self.ops[eng].append(lambda e, S, sem=sem, val=val: e.wait_ge(S[sem], val))

    def _do(self, eng, fn, inc, n):
        if inc is None:
            self.ops[eng].append(fn)
            return self.cnt[ESEM[eng]]
        self.cnt[inc] += n
        tag = self.cnt[inc]
        self.ops[eng].append(lambda e, S, fn=fn, inc=inc, n=n: fn(e, S).then_inc(S[inc], n))
        return tag

    def op(self, eng, fn, bar=False, inc="auto"):
        sem = ESEM[eng] if inc == "auto" else inc
        tag = self._do(eng, fn, sem, 1)
        if bar and sem is not None:
            self.wt(eng, sem, tag)
        return tag

    def dma(self, eng, fn, sem=None):
        if sem is None:
            sem = {"sync": "dq", "gpsimd": "dg", "scalar": "dsc"}[eng]
        return self._do(eng, fn, sem, 16)

    def cc(self, fn, sem):
        assert self.cnt[sem] == 0
        self.cnt[sem] = 1
        self.ops["gpsimd"].append(lambda e, S, fn=fn, sem=sem: fn(e, S).then_inc(S[sem]))
        return 1

    def tag(self, key):
        return self.cnt[key]

    _QISS = {"dq": "sync", "dg": "gpsimd", "dsc": "scalar"}
    _QFS = {"dq": "sq", "dg": "sg", "dsc": "ss"}

    def wtd(self, eng, qsem, tag_ignored=None):
        issuer = self._QISS[qsem]
        if eng == issuer:
            self.wt(eng, qsem, self.cnt[qsem])
            return
        if not hasattr(self, "_fcache"):
            self._fcache = {}
        cur = self.cnt[qsem]
        key = (qsem, cur)
        if key not in self._fcache:
            fs = self._QFS[qsem]
            self.wt(issuer, qsem, cur)
            self._fcache[key] = self.op(issuer, lambda e, S: e.nop(), inc=fs)
        self.wt(eng, self._QFS[qsem], self._fcache[key])

    def breg(self, e, S, name, val):
        if name not in self.bregs:
            self.bregs[name] = e.to_reg(val)
        return self.bregs[name]


class WRing:

    def __init__(self, pb):
        self.pb = pb
        self.prev = {0: None, 1: None}
        self.sem = {0: "wsa", 1: "wsb"}

    def stream(self, fn, slot, eng="gpsimd"):
        pb = self.pb
        pv = self.prev[slot]
        if pv is not None and pv[1] is not None:
            pb.wt(eng, "st", pv[1])
        tq = pb.dma(eng, fn, sem=self.sem[slot])
        rec = [tq, None]
        self.prev[slot] = rec
        return rec

    def consume(self, rec, slot, eng="tensor"):
        self.pb.wt(eng, self.sem[slot], rec[0])

    def done(self, rec, mmtag):
        rec[1] = mmtag


class Regions:
    def __init__(self, tensor, ncols):
        self.t, self.n, self.map = tensor, ncols, {}

    def alloc(self, name, cols, at=None):
        if at is None:
            at = max((a + c for (a, c) in self.map.values()), default=0)
        assert at + cols <= self.n, f"{name}: {at}+{cols} > {self.n}"
        self.map[name] = (at, cols)
        return at

    def clear(self, keep=()):
        self.map = {k: v for k, v in self.map.items() if k in keep}

    def __call__(self, name, lo=0, hi=None, p0=0, p1=128):
        a, c = self.map[name]
        if hi is None:
            hi = c
        assert 0 <= lo <= hi <= c, (name, lo, hi, c)
        return self.t[p0:p1, a + lo:a + hi]


def TT(out, a, b, op):
    return lambda e, S: e.tensor_tensor(out=out, in0=a, in1=b, op=op)


def TS(out, a, s1, op0, s2=None, op1=None):
    def f(e, S):
        kw = {}
        if op1 is not None:
            kw["scalar2"] = s2
            kw["op1"] = op1
        else:
            kw["scalar2"] = None
        return e.tensor_scalar(out=out, in0=a, scalar1=s1, op0=op0, **kw)
    return f


def STT(out, a, s, b, op0, op1):
    return lambda e, S: e.scalar_tensor_tensor(out=out, in0=a, scalar=s, in1=b,
                                               op0=op0, op1=op1)


def CP(out, a):
    return lambda e, S: e.tensor_copy(out=out, in_=a)


def SCP(out, a):
    return lambda e, S: e.copy(out=out, in_=a)


def RED(out, a, op):
    return lambda e, S: e.tensor_reduce(out=out, in_=a, axis=AXX, op=op)


def ACT(out, a, f, bias=None, scale=None):
    kw = {}
    if bias is not None:
        kw["bias"] = bias
    if scale is not None:
        kw["scale"] = scale
    return lambda e, S: e.activation(out=out, in_=a, func=f, **kw)


def MM(psum, lhsT, rhs, start, stop):
    return lambda e, S: e.matmul(psum, lhsT, rhs, start=start, stop=stop)


def TRS(psum, a, ident, kp=128):
    f = a.shape[-1]
    return lambda e, S: e.transpose(psum[0:f, 0:kp], a, ident[0:kp, 0:kp])


def MEMS(ap, v):
    return lambda e, S: e.memset(ap, v)


def RECIP(out, a):
    return lambda e, S: e.reciprocal(out=out, in_=a)


def SCANOP(out, d0, d1, init, op0=ALU.mult, op1=ALU.add):
    return lambda e, S: e.tensor_tensor_scan(out=out, data0=d0, data1=d1,
                                             initial=init, op0=op0, op1=op1)


def DMA(out, in_, **kw):
    return lambda e, S: e.dma_start(out=out, in_=in_, **kw)


TAPDEFS = {
    "ln1": ((D, NTOK), BF16), "scores": ((128, 512), F32),
    "topk_e": ((128, 48), F32), "topk_w": ((128, 48), F32),
    "lpos": ((128, 48), F32), "counts": ((1, 64), F32),
    "x_sh": ((D, NTOK), F32),
    "gidx": ((128, 8), I32), "xrow": ((128, 8), I32),
    "pdst": ((128, 64), I32), "hrow": ((128, 16), BF16),
    "x1": ((D, NTOK), F32),
    "xn2": ((128, NCH * 1024), BF16), "dtsp": ((16, NTOK), F32),
    "xbar": ((16, NTOK), F32), "yscan": ((16, NTOK), F32),
    "x2": ((D, NTOK), F32), "dbg": ((128, 64), F32),
    "dA2": ((128, 2048), F32), "uf2": ((128, 2048), F32), "sc2": ((128, 2048), F32), "ypr": ((128, 2048), F32),
    "glog": ((64, 1024), F32), "stok": ((128, 512), F32),
}


class Ctx:
    pass


def build_program(taps=(), phases=99):
    nc = bass.Bass(target_bir_lowering=False)
    pb = PB()
    ctx = Ctx()
    ctx.nc, ctx.pb, ctx.taps, ctx.phases = nc, pb, set(taps), phases

    EI = {}

    def ei(name, shape, dt):
        EI[name] = nc.dram_tensor(name, list(shape), dt, kind="ExternalInput").ap()

    ei("x_fm", (D, NTOK), F32)
    ei("wg", (128, 16 * 64), F32)
    ei("wsu", (128, 8 * 2048), F32)
    ei("wsd", (128, 16 * 512), F32)
    ei("wup", (E_OWN * 128, 4 * 2048), F32)
    ei("wdn", (E_OWN * 128, 2 * 2048), F32)
    ei("w_inT", (128, NMC * 2048), F32)
    ei("w_outT", (128, 16 * 4096), F32)
    ei("wh2i", (16, 4096), F32)
    CONSTS = [
        ("ident_f", (128, 128), F32), ("iota_col", (128, 1), F32),
        ("iota64", (128, 64), F32), ("ysum_f", (128, 32), F32),
        ("tok_base", (128, 1), F32), ("eg1024_col", (128, 1), F32),
        ("sel_own_f", (64, 8), F32),
        ("stitch_w", (8, 8), F32), ("stitch_pick", (8, 1), F32),
        ("halo_selm", (32, 4), F32), ("ones_row_f", (1, 128), F32),
        ("ones_col_f", (128, 1), F32), ("eps_ln", (128, 1), F32),
        ("eps_rms", (128, 1), F32), ("one_colf", (128, 1), F32),
        ("wnormc", (128, 16), F32), ("dtb", (16, 1), F32), ("alog", (16, 1), F32),
        ("wconv", (128, 128), F32),
    ]
    CONSTS_B = [
        ("ident_b", (128, 128), BF16), ("ones_col_b", (128, 1), BF16),
        ("ones_row_b", (1, 128), BF16), ("tri_excl_b", (128, 128), BF16),
        ("xmean_sel_b", (128, 512), BF16), ("headbc_b", (16, 256), BF16),
    ]
    for nm, shape, dt in CONSTS + CONSTS_B:
        ei(nm, shape, dt)
    ctx.EI = EI

    ctx.out_t = nc.dram_tensor("out_t", [NTOK, D], F32, kind="ExternalOutput").ap()
    ctx.tap_t = {}
    for t in ctx.taps:
        shape, dt = TAPDEFS[t]
        ctx.tap_t[t] = nc.dram_tensor("tap_" + t, list(shape), dt, kind="ExternalOutput").ap()

    dr = {}

    def di(name, shape, dt, **kw):
        dr[name] = nc.dram_tensor(name, list(shape), dt, **kw).ap()

    di("xfm_b", (D, NTOK), BF16)
    di("tag_in", (NTOK, D), BF16)
    di("tag_out", (NCORE * NTOK, D), BF16, addr_space="Shared")
    di("lloc", (E_TOT * CAP, 2), F32)
    di("lall", (NCORE * E_TOT * CAP, 2), F32, addr_space="Shared")
    di("ctloc", (1, E_TOT), F32)
    di("ctall", (NCORE, E_TOT), F32, addr_space="Shared")
    di("xdisp", (E_OWN * CAP, D), BF16)
    di("pacc0", (NCORE * NTOK, D), BF16)
    di("pacc1", (NCORE * NTOK, D), BF16)
    di("rsout0", (NTOK, D), BF16)
    di("rsout1", (NTOK, D), BF16)
    di("halo_in", (4, D), F32)
    di("halo_all", (NCORE * 4, D), F32, addr_space="Shared")
    di("sum_in", (272, 1), F32)
    di("sum_all", (NCORE * 272, 1), F32, addr_space="Shared")
    ctx.dr = dr

    for s in ("sv", "ss", "st", "sg", "sq", "dq", "dg", "dsc",
              "cc_tag0", "cc_ct0", "cc_l0", "cc_tag1", "cc_ct1", "cc_l1",
              "cc_halo", "cc_sum", "cc_rs0", "cc_rs1",
              "wsa", "wsb", "wsc", "pz", "ls",
              "ps0", "ps1", "gsA", "xg0", "xg1", "sx0", "sx1", "ot"):
        pb.sem(s)

    import contextlib
    stack = contextlib.ExitStack()
    sb = {}
    for nm, shape, dt, kw in [
        ("xs", (128, NCH * 1024), F32, {}),
        ("ab", (128, 45248), BF16, {"is_transpose_dest": True}),
        ("af", (128, 12544), F32, {}),
        ("cbf", (128, 640), F32, {}),
        ("cbb", (128, 1216), BF16, {}),
        ("ib", (128, 144), I32, {}),
    ]:
        sb[nm] = stack.enter_context(nc.sbuf_tensor(nm, list(shape), dt, **kw))
    ps = {}
    for nm, shape, dt in [
        ("p0", (128, 512), F32), ("p1", (128, 512), F32),
        ("p2", (128, 512), F32), ("p3", (128, 512), F32),
        ("p4", (128, 512), F32), ("p5", (128, 512), F32),
        ("pt", (128, 128), F32), ("ptb", (128, 128), BF16),
    ]:
        ps[nm] = stack.enter_context(nc.psum_tensor(nm, list(shape), dt))
    semh = {nm: stack.enter_context(nc.semaphore(nm)) for nm in pb.semnames}

    ctx.xs = sb["xs"]
    ctx.AB = Regions(sb["ab"], 45248)
    ctx.AF = Regions(sb["af"], 12544)
    ctx.cbf, ctx.cbb, ctx.ibuf = sb["cbf"], sb["cbb"], sb["ib"]
    ctx.ps = ps
    ctx.P = [ps[f"p{i}"] for i in range(6)]
    ctx.pt, ctx.ptb = ps["pt"], ps["ptb"]

    ctx.ring = WRing(pb)
    ctx.CF, ctx.CB = {}, {}
    col = 0
    for nm, shape, dt in CONSTS:
        ctx.CF[nm] = (col, shape[0], shape[1])
        col += shape[1]
    ctx.CF["zero_col"] = (col, 128, 1); col += 1
    assert col <= 640, col
    col = 0
    for nm, shape, dt in CONSTS_B:
        ctx.CB[nm] = (col, shape[0], shape[1])
        col += shape[1]
    ctx.CB["zero_colb"] = (col, 128, 1); col += 1
    assert col <= 1216, col

    emit_program(ctx)

    with stack:
        with nc.Block() as block:
            S = dict(semh)
            for eng in ENGINES:
                getattr(block, eng)(lambda e, _ops=pb.ops[eng], _S=S: [f(e, _S) for f in _ops])
    return nc


def cf(ctx, nm):
    a, r, c = ctx.CF[nm]
    return ctx.cbf[0:r, a:a + c]


def cb(ctx, nm):
    a, r, c = ctx.CB[nm]
    return ctx.cbb[0:r, a:a + c]


def emit_program(ctx):
    emit_p0(ctx)
    if ctx.phases >= 1:
        emit_ffn(ctx, app=0)
    if ctx.phases >= 2:
        emit_mamba(ctx)
    if ctx.phases >= 3:
        emit_ffn(ctx, app=1)
    if ctx.phases >= 4:
        emit_output(ctx)
    emit_final_wait(ctx)


def emit_p0(ctx):
    pb, EI, xs = ctx.pb, ctx.EI, ctx.xs
    for nm, (a, r, c) in ctx.CF.items():
        if nm == "zero_col":
            continue
        pb.dma("sync", DMA(ctx.cbf[0:r, a:a + c], EI[nm][:, :]))
    for nm, (a, r, c) in ctx.CB.items():
        if nm == "zero_colb":
            continue
        pb.dma("sync", DMA(ctx.cbb[0:r, a:a + c], EI[nm][:, :]))
    for c in range(NCH):
        pb.dma("sync", DMA(xs[:, c * 1024:(c + 1) * 1024],
                           EI["x_fm"][c * 128:(c + 1) * 128, :]))
    ctx.t_consts = pb.tag("dq")
    pb.op("vector", MEMS(cf(ctx, "zero_col"), 0.0))
    pb.op("vector", MEMS(cb(ctx, "zero_colb"), 0.0))
    ctx.AB.clear()
    ctx.AB.alloc("zeros", 2048, at=16384)
    ctx.AB.alloc("zeros2", 2048, at=40960)
    tz = pb.op("vector", MEMS(ctx.AB("zeros"), 0.0))
    tz2 = pb.op("vector", MEMS(ctx.AB("zeros2"), 0.0))
    pb.wt("sync", "sv", tz)
    pb.wt("gpsimd", "sv", tz)
    pb.dma("gpsimd", DMA(ctx.dr["lloc"].rearrange("(p a) c -> p (a c)", p=128),
                         ctx.AB("zeros")[:, 0:1024]))
    pb.wt("scalar", "sv", tz2)
    napps = 2 if ctx.phases >= 3 else 1
    for app in range(napps):
        pa = ctx.dr[f"pacc{app}"]
        for i in range(64):
            pb.dma("scalar", DMA(pa[i * 128:(i + 1) * 128, :], ctx.AB("zeros2")),
                   sem="pz")
    ctx.t_pzero = pb.tag("pz")



def emit_ln(ctx, out_region, wait_tags=(), rms_fuse=False):
    pb, xs, AB, AF_ = ctx.pb, ctx.xs, ctx.AB, ctx.AF
    P = ctx.P
    for eng in ("vector", "scalar", "tensor"):
        for (s, t) in wait_tags:
            if s in ("dq", "dg", "dsc"):
                pb.wtd(eng, s, t)
            else:
                pb.wt(eng, s, t)
    mm_tags = []
    for c in range(NCH):
        xc = xs[:, c * 1024:(c + 1) * 1024]
        xb = AB(out_region, c * 1024, (c + 1) * 1024)
        sq = AB("ebx", (c % 2) * 1024, (c % 2) * 1024 + 1024)
        if c >= 2:
            pb.wt("scalar", "st", mm_tags[c - 2])
        tv = pb.op("vector", CP(xb, xc))
        tsc = pb.op("scalar", ACT(sq, xc, AF.Square))
        pb.wt("tensor", "sv", tv)
        pb.wt("tensor", "ss", tsc)
        for h in (0, 1):
            pb.op("tensor", MM(P[h][0:1, 0:512], cb(ctx, "ones_col_b"),
                               xb[:, h * 512:(h + 1) * 512], c == 0, c == NCH - 1),
                  inc=(None if c < NCH - 1 else "auto"))
            t = pb.op("tensor", MM(P[2 + h][0:1, 0:512], cb(ctx, "ones_col_b"),
                                   sq[:, h * 512:(h + 1) * 512], c == 0, c == NCH - 1))
        mm_tags.append(t)
    t_sums = pb.tag("st")
    sr = lambda lo, hi: AF_("srow", lo, hi, p1=1)
    pb.wt("vector", "st", t_sums)
    pb.op("vector", CP(sr(0, 512), P[0][0:1, 0:512]))
    pb.op("vector", CP(sr(512, 1024), P[1][0:1, 0:512]), bar=True)
    pb.op("vector", TS(sr(1024, 2048), sr(0, 1024), 1.0 / D, ALU.mult), bar=True)
    pb.op("vector", CP(sr(0, 512), P[2][0:1, 0:512]))
    pb.op("vector", CP(sr(512, 1024), P[3][0:1, 0:512]), bar=True)
    pb.op("vector", TT(sr(2048, 3072), sr(1024, 2048), sr(1024, 2048), ALU.mult), bar=True)
    pb.op("vector", STT(sr(2048, 3072), sr(0, 1024), 1.0 / D, sr(2048, 3072),
                        ALU.mult, ALU.subtract), bar=True)
    if rms_fuse:
        pb.wt("scalar", "sv", pb.tag("sv"))
        t = pb.op("scalar", ACT(sr(3072, 4096), sr(2048, 3072), AF.Identity,
                                bias=cf(ctx, "eps_ln")[0:1, :]))
        pb.wt("vector", "ss", t)
        pb.op("vector", RECIP(sr(3072, 4096), sr(3072, 4096)), bar=True)
        pb.op("vector", TT(sr(3072, 4096), sr(2048, 3072), sr(3072, 4096), ALU.mult),
              bar=True)
        pb.wt("scalar", "sv", pb.tag("sv"))
        t = pb.op("scalar", ACT(sr(3072, 4096), sr(3072, 4096), AF.Sqrt,
                                bias=cf(ctx, "eps_rms")[0:1, :]))
        pb.wt("vector", "ss", t)
        pb.op("vector", RECIP(sr(3072, 4096), sr(3072, 4096)), bar=True)
    pb.wt("scalar", "sv", pb.tag("sv"))
    t = pb.op("scalar", ACT(sr(2048, 3072), sr(2048, 3072), AF.Sqrt,
                            bias=cf(ctx, "eps_ln")[0:1, :]))
    pb.wt("vector", "ss", t)
    t_rstd = pb.op("vector", RECIP(sr(2048, 3072), sr(2048, 3072)), bar=True)
    pb.wt("tensor", "sv", t_rstd)
    rows = [(1024, 0), (2048, 1024)] + ([(3072, 2048)] if rms_fuse else [])
    for (src, dst) in rows:
        for h in (0, 1):
            tmm = pb.op("tensor", MM(P[4][:, 0:512], cf(ctx, "ones_row_f"),
                                     sr(src + h * 512, src + h * 512 + 512), True, True))
            pb.wt("vector", "st", tmm)
            tv = pb.op("vector", CP(AF_("stats", dst + h * 512, dst + h * 512 + 512),
                                    P[4][:, 0:512]))
            pb.wt("tensor", "sv", tv)
    t_bc = pb.tag("sv")
    pb.wt("vector", "sv", t_bc)
    return t_bc



def emit_ffn(ctx, app):
    pb, EI, dr = ctx.pb, ctx.EI, ctx.dr
    xs, AB, AF_, ibuf = ctx.xs, ctx.AB, ctx.AF, ctx.ibuf
    P, pt = ctx.P, ctx.pt
    taps = ctx.taps if app == 0 else set()
    tap = ctx.tap_t
    cc_tag, cc_ct, cc_l = f"cc_tag{app}", f"cc_ct{app}", f"cc_l{app}"

    AB.clear()
    AB.alloc("xln", 16384, at=0)
    AB.alloc("tt", 16384, at=16384)
    AB.alloc("ysh", 4096, at=16384)
    AB.alloc("hsh", 4096, at=20480)
    AB.alloc("sg", 1024, at=24576)
    AB.alloc("wa", 2048, at=32768)
    AB.alloc("wb", 2048, at=34816)
    AB.alloc("ebx", 2048, at=36864)
    AB.alloc("gt", 4096, at=36864)
    AB.alloc("hT", 2048, at=40960)
    AB.alloc("drow", 2048, at=43008)
    AB.alloc("dr0", 2048, at=0)
    AB.alloc("dr1", 2048, at=2048)
    AF_.clear()
    AF_.alloc("stats", 3072, at=0)
    AF_.alloc("srow", 4096, at=3072)
    AF_.alloc("wgb", 1024, at=7168)
    AF_.alloc("sc", 1024, at=8192)
    AF_.alloc("stok", 512, at=7168)
    AF_.alloc("tkv", 512, at=9216)
    AF_.alloc("ew", 192, at=9728)
    AF_.alloc("posw", 512, at=9920)
    AF_.alloc("mrg", 1024, at=10432)
    AF_.alloc("t1", 1024, at=11456)
    AF_.alloc("t2", 1024, at=10432)

    wait0 = [("dq", ctx.t_consts)] if app == 0 else [("sv", ctx.x2_tag)]
    if app == 1:
        pb.wt("sync", "sv", ctx.x2_tag)
    emit_ln(ctx, "xln", wait_tags=wait0)
    mean_b = AF_("stats", 0, 1024)
    rstd_b = AF_("stats", 1024, 2048)

    t_wg = pb.dma("sync", DMA(AF_("wgb"), EI["wg"][:, :]))
    pb.wtd("tensor", "dq", t_wg)

    for c in range(NCH):
        xc = xs[:, c * 1024:(c + 1) * 1024]
        tc = AF_("t1") if c % 2 == 0 else AF_("t2")
        xnb = AB("xln", c * 1024, (c + 1) * 1024)
        pb.op("vector", TT(tc, xc, mean_b, ALU.subtract), bar=True)
        tv = pb.op("vector", TT(tc, tc, rstd_b, ALU.mult), bar=True)
        pb.wt("scalar", "sv", tv)
        tsc = pb.op("scalar", SCP(xnb, tc))
        pb.wt("tensor", "sv", tv)
        pb.wt("tensor", "ss", tsc)
        for h in (0, 1):
            t = pb.op("tensor", MM(P[4 + h][0:64, 0:512],
                                   AF_("wgb", c * 64, (c + 1) * 64),
                                   tc[:, h * 512:(h + 1) * 512], c == 0, c == NCH - 1),
                      inc=(None if (h == 0 and c < NCH - 1) else "auto"))
        pb.wt("vector", "st", t)
        pb.wt("sync", "ss", tsc)
        pb.dma("sync", DMA(dr["xfm_b"][c * 128:(c + 1) * 128, :], xnb))
    t_lnB = pb.tag("ss")
    t_gate = pb.tag("st")
    t_xfm = pb.tag("dq")
    if "ln1" in taps:
        pb.wt("sync", "ss", t_lnB)
        for c in range(NCH):
            pb.dma("sync", DMA(tap["ln1"][c * 128:(c + 1) * 128, :],
                               AB("xln", c * 1024, (c + 1) * 1024)))

    pb.wt("sync", "dq", t_xfm)
    if app == 0:
        pb.wtd("sync", "dg")
    for i in range(8):
        pb.dma("sync", DMA(AB("tt", i * 2048, (i + 1) * 2048),
                           dr["xfm_b"][:, i * 128:(i + 1) * 128], transpose=True))
    pb.wt("sync", "dq", pb.tag("dq"))
    for i in range(8):
        pb.dma("sync", DMA(dr["tag_in"][i * 128:(i + 1) * 128, :],
                           AB("tt", i * 2048, (i + 1) * 2048)))
    t_tagin = pb.tag("dq")
    pb.wtd("gpsimd", "dq", t_tagin)
    pb.cc(lambda e, S: e.collective_compute(
        "AllGather", ALU.bypass, replica_groups=[list(range(NCORE))],
        ins=[dr["tag_in"].opt()], outs=[dr["tag_out"].opt()]), cc_tag)

    if "glog" in taps:
        pb.wt("vector", "st", t_gate)
        pb.op("vector", CP(AF_("t1", 0, 512, p1=64), P[4][0:64, 0:512]))
        tvg = pb.op("vector", CP(AF_("t1", 512, 1024, p1=64), P[5][0:64, 0:512]), bar=True)
        pb.wt("sync", "sv", tvg)
        pb.dma("sync", DMA(tap["glog"][:, :], AF_("t1", 0, 1024, p1=64)))
    pb.wt("scalar", "st", t_gate)
    for h in (0, 1):
        pb.op("scalar", ACT(AF_("sc", h * 512, (h + 1) * 512, p1=64),
                            P[4 + h][0:64, 0:512], AF.Sigmoid))
    t_sig = pb.tag("ss")
    if "scores" in taps:
        pb.wt("sync", "ss", t_sig)
        pb.dma("sync", DMA(tap["scores"][0:64, :], AF_("sc", 0, 512, p1=64)))
        tq_tap = pb.dma("sync", DMA(tap["scores"][64:128, :], AF_("sc", 512, 1024, p1=64)))
        pb.wtd("vector", "dq", tq_tap)
    pb.wt("tensor", "ss", t_sig)
    for i in range(8):
        tmm = pb.op("tensor", TRS(pt[:, :], AF_("sc", i * 128, (i + 1) * 128, p1=64),
                                  cf(ctx, "ident_f"), kp=64))
        pb.wt("vector", "st", tmm)
        tv = pb.op("vector", CP(AF_("stok", i * 64, (i + 1) * 64), pt[:, 0:64]))
        pb.wt("tensor", "sv", tv)

    if "stok" in taps:
        pb.wt("sync", "sv", pb.tag("sv"))
        pb.dma("sync", DMA(tap["stok"][:, :], AF_("stok", 0, 512)))
    EWE, EWV, EWW = 0, 48, 96
    for i in range(8):
        cur = AF_("tkv", 0, 64)
        pb.op("vector", CP(cur, AF_("stok", i * 64, (i + 1) * 64)), bar=True)
        for k in range(TOPK):
            val = AF_("ew", EWV + i * 6 + k, EWV + i * 6 + k + 1)
            idx = AF_("ew", EWE + i * 6 + k, EWE + i * 6 + k + 1)
            neq = AF_("tkv", 128, 192)
            msk = AF_("tkv", 192, 256)
            pb.op("vector", RED(val, cur, ALU.max), bar=True)
            pb.op("vector", TS(neq, cur, val, ALU.not_equal), bar=True)
            pb.op("vector", STT(msk, neq, 1000.0, cf(ctx, "iota64"), ALU.mult, ALU.add), bar=True)
            pb.op("vector", RED(idx, msk, ALU.min), bar=True)
            if k < TOPK - 1:
                eqm = AF_("tkv", 256, 320)
                pb.op("vector", TS(eqm, cf(ctx, "iota64"), idx, ALU.is_equal), bar=True)
                pb.op("vector", STT(cur, eqm, -1.0e9, cur, ALU.mult, ALU.add), bar=True)
    for i in range(8):
        vs = AF_("ew", EWV + i * 6, EWV + (i + 1) * 6)
        ws = AF_("tkv", 320, 321)
        pb.op("vector", RED(ws, vs, ALU.add), bar=True)
        pb.op("vector", RECIP(ws, ws), bar=True)
        pb.op("vector", TS(AF_("ew", EWW + i * 6, EWW + (i + 1) * 6), vs, ws, ALU.mult), bar=True)
    t_topk = pb.tag("sv")
    if app == 0:
        for nm, base in (("topk_e", EWE), ("topk_w", EWW)):
            if nm in taps:
                pb.wt("sync", "sv", t_topk)
                pb.dma("sync", DMA(tap[nm][:, :], AF_("ew", base, base + 48)))

    pb.wt("tensor", "ss", t_sig)
    pb.wt("gpsimd", "dg", pb.tag("dg"))
    carry_b = AB("ebx", 0, 64, p0=0, p1=1)
    carry_f = AF_("tkv", 448, 512, p1=1)
    pb.op("vector", MEMS(carry_b, 0.0), bar=True)
    pb.op("vector", MEMS(carry_f, 0.0), bar=True)
    for i in range(8):
        pb.wt("vector", "ls", pb.tag("ls"))
        oh = AF_("posw", 0, 64)
        for k in range(TOPK):
            ecol = AF_("ew", EWE + i * 6 + k, EWE + i * 6 + k + 1)
            ohk = AF_("posw", 128, 192)
            pb.op("vector", TS(ohk, cf(ctx, "iota64"), ecol, ALU.is_equal), bar=True)
            if k == 0:
                pb.op("vector", CP(oh, ohk), bar=True)
            else:
                pb.op("vector", TT(oh, oh, ohk, ALU.add), bar=True)
        ohb = AB("ebx", 1024, 1088)
        tv = pb.op("vector", CP(ohb, oh), bar=True)
        pb.wt("tensor", "sv", tv)
        pb.op("tensor", MM(P[4][:, 0:64], cb(ctx, "tri_excl_b"), ohb, True, False), inc=None)
        pb.op("tensor", MM(P[4][:, 0:64], cb(ctx, "ones_row_b"), carry_b, False, True))
        tcar = pb.op("tensor", MM(pt[0:1, 0:64], cb(ctx, "ones_col_b"), ohb, True, True))
        pb.wt("vector", "st", tcar)
        posf = AF_("posw", 64, 128)
        pb.op("vector", CP(posf, P[4][:, 0:64]))
        pb.op("vector", TT(carry_f, carry_f, pt[0:1, 0:64], ALU.add), bar=True)
        tvc = pb.op("vector", CP(carry_b, carry_f), bar=True)
        pb.wt("tensor", "sv", tvc)
        tkid = AF_("tkv", 321, 322)
        pb.op("vector", TS(tkid, cf(ctx, "iota_col"), float(i * 128), ALU.add), bar=True)
        pb.op("vector", TT(tkid, tkid, cf(ctx, "tok_base"), ALU.add), bar=True)
        for k in range(TOPK):
            ecol = AF_("ew", EWE + i * 6 + k, EWE + i * 6 + k + 1)
            ohk = AF_("posw", 128, 192)
            pb.op("vector", TS(ohk, cf(ctx, "iota64"), ecol, ALU.is_equal), bar=True)
            tmp = AF_("posw", 192, 256)
            pb.op("vector", TT(tmp, ohk, posf, ALU.mult), bar=True)
            lp = AF_("mrg", i * 6 + k, i * 6 + k + 1)
            pb.op("vector", RED(lp, tmp, ALU.add), bar=True)
            dsti = AF_("tkv", 322, 323)
            pb.op("vector", STT(dsti, ecol, 1024.0, lp, ALU.mult, ALU.add), bar=True)
            icol = ibuf[:, k:k + 1]
            pb.op("vector", CP(icol, dsti))
            pb.op("vector", CP(AF_("tkv", 324 + k * 2, 325 + k * 2), tkid))
            tv = pb.op("vector", CP(AF_("tkv", 325 + k * 2, 326 + k * 2),
                                    AF_("ew", EWW + i * 6 + k, EWW + i * 6 + k + 1)), bar=True)
            pb.wt("gpsimd", "sv", tv)
            _rec = AF_("tkv", 324 + k * 2, 326 + k * 2)
            pb.dma("gpsimd", lambda e, S, icol=icol, _rec=_rec: e.indirect_dma_start(
                out=dr["lloc"], out_offset=IndirectOffsetOnAxis(icol, 0),
                in_=_rec, in_offset=None,
                bounds_check=pb.breg(e, S, "b_lloc", E_TOT * CAP - 1),
                oob_is_err=False), sem="ls")
    if "lpos" in taps:
        pb.wt("sync", "sv", pb.tag("sv"))
        pb.dma("sync", DMA(tap["lpos"][:, :], AF_("mrg", 0, 48)))
    tct = pb.op("vector", CP(AF_("tkv", 384, 448, p1=1), carry_f), bar=True)
    pb.wt("sync", "sv", tct)
    tq = pb.dma("sync", DMA(dr["ctloc"][:, :], AF_("tkv", 384, 448, p1=1)))
    if "counts" in taps:
        pb.dma("sync", DMA(tap["counts"][:, :], AF_("tkv", 384, 448, p1=1)))
    pb.wtd("gpsimd", "dq", tq)
    pb.wt("gpsimd", "ls", pb.tag("ls"))
    pb.cc(lambda e, S: e.collective_compute(
        "AllGather", ALU.bypass, replica_groups=[list(range(NCORE))],
        ins=[dr["ctloc"].opt()], outs=[dr["ctall"].opt()]), cc_ct)
    pb.cc(lambda e, S: e.collective_compute(
        "AllGather", ALU.bypass, replica_groups=[list(range(NCORE))],
        ins=[dr["lloc"].opt()], outs=[dr["lall"].opt()]), cc_l)

    emit_shared(ctx, app, t_lnB)
    emit_experts(ctx, app, cc_tag, cc_ct, cc_l)
    emit_routed_add(ctx, app)


def emit_shared(ctx, app, t_ln):
    pb, EI, xs, AB = ctx.pb, ctx.EI, ctx.xs, ctx.AB
    P = ctx.P
    pb.wtd("vector", "dq", pb.tag("dq"))
    pb.wtd("scalar", "dq", pb.tag("dq"))
    ring = ctx.ring
    for mc in range(8):
        w = AB("wa") if mc % 2 == 0 else AB("wb")
        rec = ring.stream(DMA(w, EI["wsu"][:, mc * 2048:(mc + 1) * 2048]), mc % 2)
        ring.consume(rec, mc % 2)
        pb.wt("tensor", "ss", t_ln)
        for h in (0, 1):
            pp = P[(mc % 2) * 2 + h]
            for kc in range(16):
                t = pb.op("tensor", MM(pp[:, 0:512], w[:, kc * 128:(kc + 1) * 128],
                                       AB("xln", kc * 1024 + h * 512, kc * 1024 + h * 512 + 512),
                                       kc == 0, kc == 15),
                          inc=(None if kc < 15 else "auto"))
        ring.done(rec, t)
        if mc < 4:
            pb.wt("vector", "st", t)
            for h in (0, 1):
                tv = pb.op("vector", CP(AB("ysh", mc * 1024 + h * 512, mc * 1024 + h * 512 + 512),
                                        P[(mc % 2) * 2 + h][:, 0:512]))
            pb.wt("tensor", "sv", tv)
        else:
            g = mc - 4
            pb.wt("scalar", "st", t)
            for h in (0, 1):
                tsc = pb.op("scalar", ACT(AB("sg", h * 512, h * 512 + 512),
                                          P[(mc % 2) * 2 + h][:, 0:512], AF.Silu))
            pb.wt("vector", "ss", tsc)
            tv = pb.op("vector", TT(AB("hsh", g * 1024, (g + 1) * 1024), AB("sg"),
                                    AB("ysh", g * 1024, (g + 1) * 1024), ALU.mult), bar=True)
            pb.wt("tensor", "sv", tv)
            pb.wt("scalar", "sv", tv)
    t_h = pb.tag("sv")
    for mc in range(16):
        w = AB("wa") if mc % 2 == 0 else AB("wb")
        rec = ring.stream(DMA(w[:, 0:512], EI["wsd"][:, mc * 512:(mc + 1) * 512]), mc % 2)
        ring.consume(rec, mc % 2)
        pb.wt("tensor", "sv", t_h)
        for h in (0, 1):
            pp = P[4 + h]
            for kc in range(4):
                t = pb.op("tensor", MM(pp[:, 0:512], w[:, kc * 128:(kc + 1) * 128],
                                       AB("hsh", kc * 1024 + h * 512, kc * 1024 + h * 512 + 512),
                                       kc == 0, kc == 3),
                          inc=(None if kc < 3 else "auto"))
        ring.done(rec, t)
        pb.wt("vector", "st", t)
        for h in (0, 1):
            xc = xs[:, mc * 1024 + h * 512:mc * 1024 + h * 512 + 512]
            tv = pb.op("vector", STT(xc, P[4 + h][:, 0:512], 0.5, xc, ALU.mult, ALU.add))
        pb.wt("tensor", "sv", tv)
    ctx.t_shared = pb.tag("sv")
    ctx.t_shared_mm = pb.tag("st")
    taps = ctx.taps if app == 0 else set()
    if "x_sh" in taps:
        pb.wt("sync", "sv", ctx.t_shared)
        for c in range(NCH):
            pb.dma("sync", DMA(ctx.tap_t["x_sh"][c * 128:(c + 1) * 128, :],
                               xs[:, c * 1024:(c + 1) * 1024]))


def emit_experts(ctx, app, cc_tag, cc_ct, cc_l):
    pb, EI, dr = ctx.pb, ctx.EI, ctx.dr
    AB, AF_, ibuf = ctx.AB, ctx.AF, ctx.ibuf
    P, pt = ctx.P, ctx.pt
    taps = ctx.taps if app == 0 else set()
    tap = ctx.tap_t
    pacc = dr[f"pacc{app}"]

    if app == 0:
        pb.op("vector", MEMS(AB("gt", 0, 4096), 0.0), bar=True)
        pb.op("vector", MEMS(AF_("mrg", 512, 640), 0.0), bar=True)
    pb.wt("sync", cc_ct, 1)
    _ctdst = AF_("mrg", 64, 72, p1=64)
    tq = pb.dma("sync", lambda e, S, _ctdst=_ctdst: e.dma_start(
        out=_ctdst,
        in_=dr["ctall"].rearrange("r e -> e r"), allow_slow_non_contiguous=True))
    pb.wtd("tensor", "dq", tq)
    t = pb.op("tensor", MM(pt[0:8, 0:8], cf(ctx, "sel_own_f"), AF_("mrg", 64, 72, p1=64),
                           True, True))
    pb.wt("vector", "st", t)
    pb.op("vector", CP(AF_("mrg", 72, 80, p1=8), pt[0:8, 0:8]), bar=True)
    pb.op("vector", MEMS(AF_("mrg", 80, 88, p1=8), 0.0), bar=True)
    pb.op("vector", SCANOP(AF_("mrg", 88, 96, p1=8), AF_("mrg", 72, 80, p1=8),
                           AF_("mrg", 80, 88, p1=8), 0.0, ALU.add, ALU.add), bar=True)
    pb.op("vector", MEMS(AF_("mrg", 112, 128, p1=8), 0.0), bar=True)
    pb.op("vector", CP(AF_("mrg", 113, 120, p1=8), AF_("mrg", 88, 95, p1=8)))
    tv = pb.op("vector", CP(AF_("mrg", 120, 121, p1=8), AF_("mrg", 95, 96, p1=8)), bar=True)
    pb.wt("tensor", "sv", tv)
    t = pb.op("tensor", TRS(pt[:, :], AF_("mrg", 112, 128, p1=8), cf(ctx, "ident_f"), kp=8))
    pb.wt("vector", "st", t)
    pb.op("vector", CP(AF_("mrg", 128, 136, p1=16), pt[0:16, 0:8]), bar=True)

    pb.wt("gpsimd", cc_l, 1)
    pb.wt("gpsimd", cc_tag, 1)
    for eo in range(E_OWN):
        colap = AF_("mrg", 128 + eo, 128 + eo + 1, p1=16)
        pb.wt("tensor", "sv", pb.tag("sv"))
        t = pb.op("tensor", TRS(pt[:, :], colap, cf(ctx, "ident_f"), kp=16))
        pb.wt("vector", "st", t)
        tcp = pb.op("vector", CP(AF_("mrg", 136, 152, p1=1), pt[0:1, 0:16]), bar=True)
        brow = AF_("mrg", 136, 144, p1=1)
        trow = AF_("mrg", 144, 145, p1=1)
        pb.wt("tensor", "sv", tcp)
        t = pb.op("tensor", MM(P[4][:, 0:8], cf(ctx, "ones_row_f"), brow, True, True))
        t2 = pb.op("tensor", MM(P[4][:, 8:9], cf(ctx, "ones_row_f"), trow, True, True))
        pb.wt("vector", "st", t2)
        bb = AF_("mrg", 152, 160)
        totc = AF_("mrg", 160, 161)
        pb.op("vector", CP(bb, P[4][:, 0:8]))
        tv = pb.op("vector", CP(totc, P[4][:, 8:9]), bar=True)
        pb.wt("tensor", "sv", tv)
        if "dbg" in taps and eo == 0 and app == 0:
            pb.wt("sync", "sv", pb.tag("sv"))
            pb.dma("sync", DMA(tap["dbg"][:, 0:8], bb))
            pb.dma("sync", lambda e, S: e.dma_start(out=tap["dbg"][:, 8:9], in_=totc, allow_slow_non_contiguous=True))
            pb.dma("sync", DMA(tap["dbg"][0:8, 16:24], AF_("mrg", 72, 80, p1=8)))
            pb.dma("sync", DMA(tap["dbg"][0:8, 24:32], AF_("mrg", 88, 96, p1=8)))
            pb.dma("sync", DMA(tap["dbg"][0:16, 32:40], AF_("mrg", 128, 136, p1=16)))
            pb.dma("sync", DMA(tap["dbg"][0:64, 40:48], AF_("mrg", 64, 72, p1=64)))
            _dbgsrc = AF_("mrg", 136, 152, p1=1)
            pb.dma("sync", lambda e, S, _d=_dbgsrc: e.dma_start(out=tap["dbg"][0:1, 48:64], in_=_d, allow_slow_non_contiguous=True))
        for tl in range(8):
            pg = AF_("mrg", 161, 162)
            pb.op("vector", TS(pg, cf(ctx, "iota_col"), float(tl * 128), ALU.add), bar=True)
            ge = AF_("mrg", 162, 170)
            pb.op("vector", TS(ge, bb, pg, ALU.is_le), bar=True)
            gb = AF_("mrg", 170, 178)
            pb.op("vector", TT(gb, ge, bb, ALU.mult), bar=True)
            rof = AF_("mrg", 178, 179)
            bsel = AF_("mrg", 179, 180)
            pb.op("vector", RED(rof, ge, ALU.add), bar=True)
            pb.op("vector", RED(bsel, gb, ALU.max), bar=True)
            pb.op("vector", TS(rof, rof, 1.0, ALU.subtract), bar=True)
            jj = AF_("mrg", 180, 181)
            pb.op("vector", TT(jj, pg, bsel, ALU.subtract), bar=True)
            gx = AF_("mrg", 181, 182)
            pb.op("vector", TS(gx, rof, 65536.0, ALU.mult, float(eo) * 1024.0, ALU.add), bar=True)
            pb.op("vector", TT(gx, gx, cf(ctx, "eg1024_col"), ALU.add), bar=True)
            pb.op("vector", TT(gx, gx, jj, ALU.add), bar=True)
            mval = AF_("mrg", 182, 183)
            pb.op("vector", TS(mval, pg, totc, ALU.is_ge), bar=True)
            pb.op("vector", STT(gx, mval, BIG, gx, ALU.mult, ALU.add), bar=True)
            icol = ibuf[:, 8 + tl:8 + tl + 1]
            tv = pb.op("vector", CP(icol, gx), bar=True)
            pb.wt("gpsimd", "sv", tv)
            tw2 = AF_("mrg", 512 + eo * 16 + tl * 2, 512 + eo * 16 + tl * 2 + 2)
            tqg = pb.dma("gpsimd", lambda e, S, icol=icol, tw2=tw2: e.indirect_dma_start(
                out=tw2, out_offset=None,
                in_=dr["lall"], in_offset=IndirectOffsetOnAxis(icol, 0),
                bounds_check=pb.breg(e, S, "b_lall", NCORE * E_TOT * CAP - 1),
                oob_is_err=False), sem="gsA")
            pb.wt("vector", "gsA", tqg)
            xr = AF_("mrg", 183, 184)
            tkc = AF_("mrg", 512 + eo * 16 + tl * 2, 512 + eo * 16 + tl * 2 + 1)
            pb.op("vector", STT(xr, mval, BIG2, tkc, ALU.mult, ALU.add), bar=True)
            xcol = ibuf[:, 16 + tl:16 + tl + 1]
            pdcol = ibuf[:, 64 + eo * 8 + tl:64 + eo * 8 + tl + 1]
            pb.op("vector", CP(xcol, xr))
            tv = pb.op("vector", CP(pdcol, xr), bar=True)
            pb.wt("gpsimd", "sv", tv)
            p2 = tl % 2
            xgp, sxp = ("xg0", "xg1")[p2], ("sx0", "sx1")[p2]
            gtile = AB("gt", p2 * 2048, p2 * 2048 + 2048)
            pb.wt("gpsimd", sxp, pb.tag(sxp))
            tqx = pb.dma("gpsimd", lambda e, S, xcol=xcol, gtile=gtile: e.indirect_dma_start(
                out=gtile, out_offset=None,
                in_=dr["tag_out"], in_offset=IndirectOffsetOnAxis(xcol, 0),
                bounds_check=pb.breg(e, S, "b_tag", NCORE * NTOK - 1),
                oob_is_err=False), sem=xgp)
            pb.wt("sync", xgp, tqx)
            pb.dma("sync", DMA(dr["xdisp"][eo * CAP + tl * 128:eo * CAP + (tl + 1) * 128, :],
                               gtile), sem=sxp)
        if app == 0 and eo == 0:
            if "gidx" in taps:
                pb.wt("sync", "sv", pb.tag("sv"))
                pb.dma("sync", DMA(tap["gidx"][:, :], ibuf[:, 8:16]))
            if "xrow" in taps:
                pb.wt("sync", "sv", pb.tag("sv"))
                pb.dma("sync", DMA(tap["xrow"][:, :], ibuf[:, 16:24]))
    if "pdst" in taps:
        pb.wt("sync", "sv", pb.tag("sv"))
        pb.dma("sync", DMA(tap["pdst"][:, :], ibuf[:, 64:128]))

    t_sx0, t_sx1 = pb.tag("sx0"), pb.tag("sx1")

    ring = ctx.ring
    up_done = 0
    for eo in range(E_OWN):
        pb.wt("sync", "sx0", t_sx0)
        pb.wt("sync", "sx1", t_sx1)
        pb.wt("sync", "st", ctx.t_shared_mm)
        if eo > 0:
            pb.wt("sync", "st", up_done)
        for c in range(NCH):
            pb.dma("sync", DMA(AB("tt", c * 1024, (c + 1) * 1024),
                               dr["xdisp"][eo * CAP:(eo + 1) * CAP, c * 128:(c + 1) * 128],
                               transpose=True))
        t_xeT = pb.tag("dq")
        pb.wtd("tensor", "dq", t_xeT)
        for mc in range(4):
            w = AB("wa") if mc % 2 == 0 else AB("wb")
            rec = ring.stream(DMA(
                w, EI["wup"][eo * 128:(eo + 1) * 128, mc * 2048:(mc + 1) * 2048]), mc % 2)
            ring.consume(rec, mc % 2)
            for nh in (0, 1):
                pp = P[(mc % 2) * 2 + nh]
                for kc in range(16):
                    t = pb.op("tensor", MM(pp[:, 0:512], w[:, kc * 128:(kc + 1) * 128],
                                           AB("tt", kc * 1024 + nh * 512,
                                              kc * 1024 + nh * 512 + 512),
                                           kc == 0, kc == 15),
                              inc=(None if kc < 15 else "auto"))
            t_up = pb.tag("st")
            ring.done(rec, t_up)
            if mc < 2:
                pb.wt("scalar", "st", t_up)
                for nh in (0, 1):
                    tsc = pb.op("scalar", ACT(AB("gt", mc * 1024 + nh * 512,
                                                 mc * 1024 + nh * 512 + 512),
                                              P[(mc % 2) * 2 + nh][:, 0:512], AF.Silu))
                pb.wt("tensor", "ss", tsc)
            else:
                pb.wt("vector", "st", t_up)
                for nh in (0, 1):
                    tvv = pb.op("vector", CP(AB("gt", mc * 1024 + nh * 512,
                                                mc * 1024 + nh * 512 + 512),
                                             P[(mc % 2) * 2 + nh][:, 0:512]))
                pb.wt("tensor", "sv", tvv)
        t_gv = pb.tag("sv")
        t_gs = pb.tag("ss")
        if eo == 0:
            pb.wt("vector", "pz", ctx.t_pzero)
        pb.wt("vector", "ss", t_gs)
        pb.wt("vector", "sv", t_gv)
        for c2 in (0, 1):
            tvh = pb.op("vector", TT(AB("hT", c2 * 1024, (c2 + 1) * 1024),
                                     AB("gt", c2 * 1024, (c2 + 1) * 1024),
                                     AB("gt", 2048 + c2 * 1024, 2048 + (c2 + 1) * 1024),
                                     ALU.mult), bar=True)
        pb.wt("tensor", "sv", tvh)
        up_done = pb.tag("st")
        recd = ring.stream(DMA(AB("wa"), EI["wdn"][eo * 128:(eo + 1) * 128, 0:2048]), 0)
        recd2 = ring.stream(DMA(AB("wb"), EI["wdn"][eo * 128:(eo + 1) * 128, 2048:4096]), 1)
        ring.consume(recd, 0)
        ring.consume(recd2, 1)
        t_ps_prev = (pb.tag("ps0"), pb.tag("ps1"))
        for rt in range(8):
            psem = ("ps0", "ps1")[rt % 2]
            drb = AB("dr0") if rt % 2 == 0 else AB("dr1")
            first_evac = True
            for nh4 in range(4):
                pp = P[4 + nh4 % 2]
                for kc in (0, 1):
                    w = AB("wa") if kc == 0 else AB("wb")
                    t = pb.op("tensor", MM(pp[:, 0:512],
                                           AB("hT", kc * 1024 + rt * 128,
                                              kc * 1024 + (rt + 1) * 128),
                                           w[:, nh4 * 512:(nh4 + 1) * 512], kc == 0, kc == 1),
                              inc=(None if kc == 0 else "auto"))
                pb.wt("vector", "st", t)
                if first_evac:
                    pb.wt("vector", psem, pb.tag(psem))
                    first_evac = False
                wcolap = AF_("mrg", 512 + eo * 16 + rt * 2 + 1, 512 + eo * 16 + rt * 2 + 2)
                tv = pb.op("vector", TS(drb[:, nh4 * 512:(nh4 + 1) * 512],
                                        pp[:, 0:512], wcolap, ALU.mult), bar=True)
                pb.wt("tensor", "sv", tv)
            pdcol = ibuf[:, 64 + eo * 8 + rt:64 + eo * 8 + rt + 1]
            pb.wt("gpsimd", "sv", pb.tag("sv"))
            if rt == 0:
                if eo == 0:
                    pb.wt("gpsimd", "pz", ctx.t_pzero)
                pb.wt("gpsimd", "ps0", t_ps_prev[0])
                pb.wt("gpsimd", "ps1", t_ps_prev[1])
            pb.dma("gpsimd", lambda e, S, pdcol=pdcol, _drow=drb: e.indirect_dma_start(
                out=pacc, out_offset=IndirectOffsetOnAxis(pdcol, 0),
                in_=_drow, in_offset=None,
                bounds_check=pb.breg(e, S, "b_pacc", NCORE * NTOK - 1),
                oob_is_err=False, compute_op=ALU.add), sem=psem)
        t_dn = pb.tag("st")
        ring.done(recd, t_dn)
        ring.done(recd2, t_dn)
        if "hrow" in taps and eo == 0:
            pb.wt("sync", "sv", pb.tag("sv"))
            pb.dma("sync", DMA(tap["hrow"][:, :], AB("hT", 0, 16)))
    pb.wt("gpsimd", "ps0", pb.tag("ps0"))
    pb.wt("gpsimd", "ps1", pb.tag("ps1"))
    pb.cc(lambda e, S: e.collective_compute(
        "ReduceScatter", ALU.add, replica_groups=[list(range(NCORE))],
        ins=[pacc.opt()], outs=[dr[f"rsout{app}"].opt()]), f"cc_rs{app}")


def emit_routed_add(ctx, app):
    pb, dr, xs, AB, AF_ = ctx.pb, ctx.dr, ctx.xs, ctx.AB, ctx.AF
    pb.wt("sync", f"cc_rs{app}", 1)
    for c in range(NCH):
        xgp = ("xg0", "xg1")[c % 2]
        tile = AB("gt", (c % 2) * 2048, (c % 2) * 2048 + 1024)
        pb.wt("sync", "sv", pb.tag("sv"))
        tq = pb.dma("sync", DMA(tile, dr[f"rsout{app}"][:, c * 128:(c + 1) * 128],
                                transpose=True), sem=xgp)
        pb.wt("vector", xgp, tq)
        xc = xs[:, c * 1024:(c + 1) * 1024]
        f32t = AF_("t1") if c % 2 == 0 else AF_("t2")
        pb.op("vector", CP(f32t, tile), bar=True)
        pb.op("vector", STT(xc, f32t, 0.5, xc, ALU.mult, ALU.add), bar=True)
    tag = pb.tag("sv")
    if app == 0:
        ctx.x1_tag = tag
        if "x1" in ctx.taps:
            pb.wt("sync", "sv", tag)
            for c in range(NCH):
                pb.dma("sync", DMA(ctx.tap_t["x1"][c * 128:(c + 1) * 128, :],
                                   xs[:, c * 1024:(c + 1) * 1024]))
    else:
        ctx.x3_tag = tag



def emit_mamba(ctx):
    pb, EI, dr = ctx.pb, ctx.EI, ctx.dr
    xs, AB, AF_ = ctx.xs, ctx.AB, ctx.AF
    P, pt, ptb = ctx.P, ctx.pt, ctx.ptb
    taps = ctx.taps
    tap = ctx.tap_t

    pb.wt("vector", "pz", ctx.t_pzero)
    pb.wt("scalar", "pz", ctx.t_pzero)

    AB.clear()
    AB.alloc("xn2", 16384, at=0)
    AB.alloc("yg", 16384, at=16384)
    AB.alloc("ebx", 2048, at=16384)
    AB.alloc("wa", 2048, at=32768)
    AB.alloc("wb", 2048, at=34816)
    AB.alloc("xi0", 1056, at=36864)
    AB.alloc("xi1", 1056, at=37920)
    AB.alloc("cm", 5120, at=38976)
    AB.alloc("ldb", 1024, at=44096)
    AB.alloc("xh", 64, at=45120)
    AB.alloc("wh2ib", 4096, at=36864)
    AB.alloc("zwork", 2048, at=40960)
    AF_.clear()
    AF_.alloc("stats", 3072, at=0)
    AF_.alloc("srow", 4096, at=3072)
    AF_.alloc("hal", 512, at=7168)
    AF_.alloc("t1", 1024, at=8192)
    AF_.alloc("t2", 1024, at=9216)
    AF_.alloc("Bf", 2048, at=0)
    AF_.alloc("Cf", 2048, at=2048)
    AF_.alloc("dA", 2048, at=4096)
    AF_.alloc("uf", 2048, at=6144)
    AF_.alloc("sc1", 2048, at=8192)
    AF_.alloc("s1", 1024, at=10240)
    AF_.alloc("s2", 1024, at=4096)
    AF_.alloc("s3", 1024, at=11264)
    AF_.alloc("halall", 2048, at=10240)
    AF_.alloc("srow2", 128, at=12288)

    x1t = ctx.x1_tag
    pb.wt("tensor", "sv", x1t)
    for c in range(NCH):
        t = pb.op("tensor", TRS(pt[:, :], xs[:, c * 1024 + 1020:c * 1024 + 1024],
                                cf(ctx, "ident_f")))
        pb.wt("vector", "st", t)
        tv = pb.op("vector", CP(AF_("Bf", c * 128, (c + 1) * 128, p1=4),
                                pt[0:4, 0:128]), bar=True)
        pb.wt("tensor", "sv", tv)
    t_hext = pb.tag("sv")
    pb.wt("sync", "sv", t_hext)
    th = pb.dma("sync", DMA(dr["halo_in"][:, :], AF_("Bf", 0, 2048, p1=4)))
    pb.wtd("vector", "dq", th)
    pb.wtd("gpsimd", "dq", th)
    pb.cc(lambda e, S: e.collective_compute(
        "AllGather", ALU.bypass, replica_groups=[list(range(NCORE))],
        ins=[dr["halo_in"].opt()], outs=[dr["halo_all"].opt()]), "cc_halo")

    emit_ln(ctx, "xn2", wait_tags=[("sv", x1t)], rms_fuse=True)
    mean_b = AF_("stats", 0, 1024)
    rstd_b = AF_("stats", 1024, 2048)
    rms_b = AF_("stats", 2048, 3072)
    for c in range(NCH):
        xc = xs[:, c * 1024:(c + 1) * 1024]
        tc = AF_("t1") if c % 2 == 0 else AF_("t2")
        pb.op("vector", STT(tc, xc, 1.0, mean_b, ALU.mult, ALU.subtract), bar=True)
        pb.op("vector", TT(tc, tc, rstd_b, ALU.mult), bar=True)
        pb.op("vector", TT(xc, xc, tc, ALU.add), bar=True)
        pb.op("vector", TT(tc, tc, rms_b, ALU.mult), bar=True)
        wcol = ctx.cbf[0:128, ctx.CF["wnormc"][0] + c:ctx.CF["wnormc"][0] + c + 1]
        tv = pb.op("vector", TS(AB("xn2", c * 1024, (c + 1) * 1024), tc, wcol, ALU.mult),
                   bar=True)
    t_xn2 = pb.tag("sv")
    ctx.x15_tag = t_xn2
    if "xn2" in taps:
        pb.wt("sync", "sv", t_xn2)
        pb.dma("sync", DMA(tap["xn2"][:, :], AB("xn2")))

    pb.wt("sync", "cc_halo", 1)
    th2 = pb.dma("sync", DMA(AF_("halall", 0, 2048, p1=32), dr["halo_all"][:, :]))
    pb.wtd("tensor", "dq", th2)
    t = None
    for h4 in range(4):
        t = pb.op("tensor", MM(P[0][0:4, 0:512], cf(ctx, "halo_selm"),
                               AF_("halall", h4 * 512, (h4 + 1) * 512, p1=32),
                               True, True))
        pb.wt("vector", "st", t)
        tv = pb.op("vector", CP(AF_("uf", h4 * 512, (h4 + 1) * 512, p1=4), P[0][0:4, 0:512]),
                   bar=True)
        pb.wt("tensor", "sv", tv)
    for c in range(NCH):
        t = pb.op("tensor", TRS(pt[:, :], AF_("uf", c * 128, (c + 1) * 128, p1=4),
                                cf(ctx, "ident_f"), kp=4))
        pb.wt("vector", "st", t)
        tv = pb.op("vector", CP(AF_("hal", c * 4, (c + 1) * 4), pt[:, 0:4]), bar=True)
        pb.wt("tensor", "sv", tv)
        pb.wt("scalar", "sv", tv)
    for c in range(NCH):
        tsq = pb.op("scalar", ACT(AF_("hal", 64 + c * 4, 64 + (c + 1) * 4),
                                  AF_("hal", c * 4, (c + 1) * 4), AF.Square))
        pb.wt("tensor", "ss", tsq)
        pb.op("tensor", MM(pt[0:1, 0:4], cf(ctx, "ones_col_f"),
                           AF_("hal", c * 4, (c + 1) * 4), c == 0, c == NCH - 1),
              inc=(None if c < NCH - 1 else "auto"))
        t = pb.op("tensor", MM(P[1][0:1, 0:4], cf(ctx, "ones_col_f"),
                               AF_("hal", 64 + c * 4, 64 + (c + 1) * 4), c == 0, c == NCH - 1))
    pb.wt("vector", "st", t)
    hr = lambda lo, hi: AF_("hal", lo, hi, p1=1)
    pb.op("vector", CP(hr(128, 132), pt[0:1, 0:4]))
    pb.op("vector", CP(hr(132, 136), P[1][0:1, 0:4]), bar=True)
    pb.op("vector", TS(hr(136, 140), hr(128, 132), 1.0 / D, ALU.mult), bar=True)
    pb.op("vector", TT(hr(144, 148), hr(136, 140), hr(136, 140), ALU.mult), bar=True)
    pb.op("vector", STT(hr(144, 148), hr(132, 136), 1.0 / D, hr(144, 148),
                        ALU.mult, ALU.subtract), bar=True)
    pb.wt("scalar", "sv", pb.tag("sv"))
    t = pb.op("scalar", ACT(hr(148, 152), hr(144, 148), AF.Identity,
                            bias=cf(ctx, "eps_ln")[0:1, :]))
    pb.wt("vector", "ss", t)
    pb.op("vector", RECIP(hr(148, 152), hr(148, 152)), bar=True)
    pb.op("vector", TT(hr(148, 152), hr(144, 148), hr(148, 152), ALU.mult), bar=True)
    pb.wt("scalar", "sv", pb.tag("sv"))
    t = pb.op("scalar", ACT(hr(148, 152), hr(148, 152), AF.Sqrt,
                            bias=cf(ctx, "eps_rms")[0:1, :]))
    pb.wt("vector", "ss", t)
    pb.op("vector", RECIP(hr(148, 152), hr(148, 152)), bar=True)
    pb.wt("scalar", "sv", pb.tag("sv"))
    t = pb.op("scalar", ACT(hr(144, 148), hr(144, 148), AF.Sqrt,
                            bias=cf(ctx, "eps_ln")[0:1, :]))
    pb.wt("vector", "ss", t)
    pb.op("vector", RECIP(hr(144, 148), hr(144, 148)), bar=True)
    pb.op("vector", TT(hr(144, 148), hr(144, 148), hr(148, 152), ALU.mult), bar=True)
    t_hrow = pb.tag("sv")
    pb.wt("tensor", "sv", t_hrow)
    t = pb.op("tensor", MM(P[0][:, 0:4], cf(ctx, "ones_row_f"), hr(136, 140), True, True),
              inc=None)
    t = pb.op("tensor", MM(P[0][:, 4:8], cf(ctx, "ones_row_f"), hr(144, 148), True, True))
    pb.wt("vector", "st", t)
    pb.op("vector", CP(AF_("hal", 152, 156), P[0][:, 0:4]))
    pb.op("vector", CP(AF_("hal", 156, 160), P[0][:, 4:8]), bar=True)
    for c in range(NCH):
        hx = AF_("hal", c * 4, (c + 1) * 4)
        tmp4 = AF_("hal", 160, 164)
        pb.op("vector", TT(tmp4, hx, AF_("hal", 152, 156), ALU.subtract), bar=True)
        pb.op("vector", TT(tmp4, tmp4, AF_("hal", 156, 160), ALU.mult), bar=True)
        wcol = ctx.cbf[0:128, ctx.CF["wnormc"][0] + c:ctx.CF["wnormc"][0] + c + 1]
        tv = pb.op("vector", TS(AB("xh", c * 4, (c + 1) * 4), tmp4, wcol, ALU.mult), bar=True)
    t_xh = pb.tag("sv")

    ring = ctx.ring

    def stream_w(mc, buf):
        slot = 0 if buf == "wa" else 1
        rec = ring.stream(DMA(AB(buf), EI["w_inT"][:, mc * 2048:(mc + 1) * 2048]), slot)
        ring.consume(rec, slot)
        return rec

    pb.wt("tensor", "sv", t_xn2)
    pb.wt("tensor", "sv", t_xh)

    def proj_mm(mc, buf, pdst0, pdst1, with_halo=False, hal_psum=None):
        w = AB(buf)
        for kc in range(16):
            pb.op("tensor", MM(pdst0, w[:, kc * 128:(kc + 1) * 128],
                               AB("xn2", kc * 1024, kc * 1024 + 512), kc == 0, kc == 15),
                  inc=(None if kc < 15 else "auto"))
            t = pb.op("tensor", MM(pdst1, w[:, kc * 128:(kc + 1) * 128],
                                   AB("xn2", kc * 1024 + 512, kc * 1024 + 1024),
                                   kc == 0, kc == 15),
                      inc=(None if kc < 15 else "auto"))
            if with_halo:
                t = pb.op("tensor", MM(hal_psum, w[:, kc * 128:(kc + 1) * 128],
                                       AB("xh", kc * 4 + 1, kc * 4 + 4), kc == 0, kc == 15),
                          inc=(None if kc < 15 else "auto"))
        return t

    rec = stream_w(64, "wa")
    t = proj_mm(64, "wa", P[0][:, 0:512], P[1][:, 0:512])
    ring.done(rec, t)
    pb.wt("vector", "st", t)
    pb.op("vector", CP(AF_("s1", 0, 512, p1=16), P[0][0:16, 0:512]))
    pb.op("vector", CP(AF_("s1", 512, 1024, p1=16), P[1][0:16, 0:512]), bar=True)
    t_dtraw = pb.tag("sv")
    dsts = [("Bf", 0), ("Bf", 1024), ("Cf", 0), ("Cf", 1024)]
    for j, mc in enumerate((65, 66, 67, 68)):
        buf = "wb" if j % 2 == 0 else "wa"
        rec = stream_w(mc, buf)
        t = proj_mm(mc, buf, P[2][:, 0:512], P[3][:, 0:512])
        ring.done(rec, t)
        pb.wt("vector", "st", t)
        nm, off = dsts[j]
        pb.op("vector", CP(AF_(nm, off, off + 512), P[2][:, 0:512]))
        tv = pb.op("vector", CP(AF_(nm, off + 512, off + 1024), P[3][:, 0:512]), bar=True)
        pb.wt("tensor", "sv", tv)
    for ci in range(32):
        mc = 32 + ci
        buf = "wa" if ci % 2 == 0 else "wb"
        rec = stream_w(mc, buf)
        t = proj_mm(mc, buf, P[ci % 2][:, 0:512], P[2 + ci % 2][:, 0:512],
                    with_halo=True, hal_psum=pt[:, 0:3])
        ring.done(rec, t)
        xi = AB("xi0") if ci % 2 == 0 else AB("xi1")
        xi_t = ctx.AB.t
        a0 = ctx.AB.map["xi0" if ci % 2 == 0 else "xi1"][0]
        pb.wt("vector", "st", t)
        pb.op("vector", CP(xi_t[:, a0:a0 + 3], pt[:, 0:3]))
        pb.op("vector", CP(xi_t[:, a0 + 3:a0 + 515], P[ci % 2][:, 0:512]))
        tv = pb.op("vector", CP(xi_t[:, a0 + 515:a0 + 1027], P[2 + ci % 2][:, 0:512]),
                   bar=True)
        pb.wt("tensor", "sv", tv)
        wc0 = ctx.CF["wconv"][0]
        for k in range(4):
            wcol = ctx.cbf[0:128, wc0 + ci * 4 + k:wc0 + ci * 4 + k + 1]
            pb.op("vector", TS(AB("cm", k * 1024, (k + 1) * 1024),
                               xi_t[:, a0 + k:a0 + k + 1024], wcol, ALU.mult))
        pb.wt("vector", "sv", pb.tag("sv"))
        pb.op("vector", TT(AB("cm", 0, 1024), AB("cm", 0, 1024), AB("cm", 1024, 2048),
                           ALU.add))
        pb.op("vector", TT(AB("cm", 2048, 3072), AB("cm", 2048, 3072), AB("cm", 3072, 4096),
                           ALU.add), bar=True)
        tv = pb.op("vector", TT(AB("cm", 0, 1024), AB("cm", 0, 1024), AB("cm", 2048, 3072),
                                ALU.add), bar=True)
        pb.wt("scalar", "sv", tv)
        tsc = pb.op("scalar", ACT(AB("cm", 4096, 5120), AB("cm", 0, 1024), AF.Silu))
        pb.wt("tensor", "ss", tsc)
        for h in (0, 1):
            t = pb.op("tensor", MM(P[4 + h][0:16, 0:512],
                                   cb(ctx, "xmean_sel_b")[:, ci * 16:(ci + 1) * 16],
                                   AB("cm", 4096 + h * 512, 4096 + h * 512 + 512),
                                   ci == 0, ci == 31),
                      inc=(None if h == 0 else "auto"))
        t_cm = pb.tag("st")
        pb.wt("vector", "st", t_cm)
    t_xm = pb.tag("st")
    pb.wt("vector", "st", t_xm)
    pb.op("vector", CP(AF_("s2", 0, 512, p1=16), P[4][0:16, 0:512]))
    pb.op("vector", CP(AF_("s2", 512, 1024, p1=16), P[5][0:16, 0:512]), bar=True)
    if "xbar" in taps:
        pb.wt("sync", "sv", pb.tag("sv"))
        pb.dma("sync", DMA(tap["xbar"][:, :], AF_("s2", 0, 1024, p1=16)))

    s1 = lambda lo, hi: AF_("s1", lo, hi, p1=16)
    s2 = lambda lo, hi: AF_("s2", lo, hi, p1=16)
    s3 = lambda lo, hi: AF_("s3", lo, hi, p1=16)
    dtbc = ctx.cbf[0:16, ctx.CF["dtb"][0]:ctx.CF["dtb"][0] + 1]
    pb.wt("scalar", "sv", pb.tag("sv"))
    ta = pb.op("scalar", ACT(s3(0, 1024), s1(0, 1024), AF.Abs, bias=dtbc))
    pb.wt("scalar", "ss", ta)
    ta = pb.op("scalar", ACT(s3(0, 1024), s3(0, 1024), AF.Exp, scale=-1.0))
    pb.wt("scalar", "ss", ta)
    ta = pb.op("scalar", ACT(s3(0, 1024), s3(0, 1024), AF.Ln,
                             bias=cf(ctx, "one_colf")[0:16, :]))
    pb.wt("scalar", "ss", ta)
    ta = pb.op("scalar", ACT(s1(0, 1024), s1(0, 1024), AF.Relu, bias=dtbc))
    pb.wt("vector", "ss", ta)
    pb.op("vector", TT(s1(0, 1024), s1(0, 1024), s3(0, 1024), ALU.add), bar=True)
    if "dtsp" in taps:
        pb.wt("sync", "sv", pb.tag("sv"))
        pb.dma("sync", DMA(tap["dtsp"][:, :], s1(0, 1024)))
    acol = AF_("srow2", 0, 1, p1=16)
    alogc = ctx.cbf[0:16, ctx.CF["alog"][0]:ctx.CF["alog"][0] + 1]
    ta = pb.op("scalar", ACT(acol, alogc, AF.Exp))
    pb.wt("vector", "ss", ta)
    pb.op("vector", TS(acol, acol, -1.0, ALU.mult), bar=True)
    pb.op("vector", TT(s3(0, 1024), s1(0, 1024), s2(0, 1024), ALU.mult), bar=True)
    pb.op("vector", TS(s1(0, 1024), s1(0, 1024), acol, ALU.mult), bar=True)
    pb.op("vector", RED(AF_("srow2", 1, 2, p1=16), s1(0, 1024), ALU.add), bar=True)
    tv = pb.op("vector", CP(AB("ldb", 0, 1024, p1=16), s1(0, 1024)), bar=True)
    tv2 = pb.op("vector", CP(AB("cm", 0, 1024, p1=16), s3(0, 1024)), bar=True)
    pb.wt("tensor", "sv", tv2)
    pb.wtd("scalar", "dq")
    pb.wtd("vector", "dq")
    for ci in (0, 1):
        hb = cb(ctx, "headbc_b")[:, ci * 128:(ci + 1) * 128]
        for h in (0, 1):
            t = pb.op("tensor", MM(P[0][:, 0:512], hb,
                                   AB("ldb", h * 512, h * 512 + 512, p1=16), True, True))
            t2 = pb.op("tensor", MM(P[1][:, 0:512], hb,
                                    AB("cm", h * 512, h * 512 + 512, p1=16), True, True))
            pb.wt("scalar", "st", t)
            tsc = pb.op("scalar", ACT(AF_("dA", ci * 1024 + h * 512, ci * 1024 + h * 512 + 512),
                                      P[0][:, 0:512], AF.Exp))
            pb.wt("vector", "st", t2)
            pb.op("vector", CP(AF_("uf", ci * 1024 + h * 512, ci * 1024 + h * 512 + 512),
                               P[1][:, 0:512]), bar=True)
            pb.wt("tensor", "sv", pb.tag("sv"))
            pb.wt("tensor", "ss", tsc)
        pb.op("vector", TT(AF_("uf", ci * 1024, (ci + 1) * 1024),
                           AF_("uf", ci * 1024, (ci + 1) * 1024),
                           AF_("Bf", ci * 1024, (ci + 1) * 1024), ALU.mult), bar=True)
    t_scanin = pb.tag("sv")
    pb.wt("vector", "ss", pb.tag("ss"))
    if "dA2" in taps:
        pb.wt("sync", "sv", pb.tag("sv"))
        pb.wt("sync", "ss", pb.tag("ss"))
        pb.dma("sync", DMA(tap["dA2"][:, :], AF_("dA")))
        pb.dma("sync", DMA(tap["uf2"][:, :], AF_("uf")))
    for ci in (0, 1):
        pb.op("vector", SCANOP(AF_("sc1", ci * 1024, (ci + 1) * 1024),
                               AF_("dA", ci * 1024, (ci + 1) * 1024),
                               AF_("uf", ci * 1024, (ci + 1) * 1024), 0.0), bar=True)
    pb.wt("sync", "sv", pb.tag("sv"))
    pb.dma("sync", DMA(dr["sum_in"][0:16, :], AF_("srow2", 1, 2, p1=16)))
    pb.dma("sync", DMA(dr["sum_in"][16:144, :], AF_("sc1", 1023, 1024)))
    tq = pb.dma("sync", DMA(dr["sum_in"][144:272, :], AF_("sc1", 2047, 2048)))
    pb.wtd("gpsimd", "dq", tq)
    pb.cc(lambda e, S: e.collective_compute(
        "AllGather", ALU.bypass, replica_groups=[list(range(NCORE))],
        ins=[dr["sum_in"].opt()], outs=[dr["sum_all"].opt()]), "cc_sum")
    pb.wt("sync", "cc_sum", 1)
    for r in range(NCORE):
        pb.dma("sync", DMA(AF_("srow2", 8 + r, 9 + r, p1=16),
                           dr["sum_all"][r * 272:r * 272 + 16, :]))
        pb.dma("sync", DMA(AF_("srow2", 16 + r, 17 + r, p1=128),
                           dr["sum_all"][r * 272 + 16:r * 272 + 144, :]))
        tq = pb.dma("sync", DMA(AF_("srow2", 24 + r, 25 + r, p1=128),
                                dr["sum_all"][r * 272 + 144:r * 272 + 272, :]))
    pb.wtd("tensor", "dq", tq)
    t = pb.op("tensor", TRS(pt[:, :], AF_("srow2", 8, 16, p1=16), cf(ctx, "ident_f"), kp=16))
    pb.wt("vector", "st", t)
    pb.op("vector", CP(AF_("srow2", 32, 48, p1=8), pt[0:8, 0:16]), bar=True)
    pb.wt("tensor", "sv", pb.tag("sv"))
    t = pb.op("tensor", MM(P[0][0:8, 0:16], cf(ctx, "stitch_w"), AF_("srow2", 32, 48, p1=8),
                           True, True))
    pb.wt("scalar", "st", t)
    t = pb.op("scalar", ACT(AF_("srow2", 48, 64, p1=8), P[0][0:8, 0:16], AF.Exp))
    pb.wt("vector", "ss", t)
    pb.op("vector", TS(AF_("srow2", 48, 64, p1=8), AF_("srow2", 48, 64, p1=8),
                       cf(ctx, "stitch_pick"), ALU.mult), bar=True)
    pb.wt("tensor", "sv", pb.tag("sv"))
    t = pb.op("tensor", TRS(pt[:, :], AF_("srow2", 48, 64, p1=8), cf(ctx, "ident_f"), kp=8))
    pb.wt("vector", "st", t)
    pb.op("vector", CP(AF_("srow2", 64, 72, p1=16), pt[0:16, 0:8]), bar=True)
    pb.op("vector", CP(AB("ldb", 0, 8, p1=16), AF_("srow2", 64, 72, p1=16)), bar=True)
    pb.wt("tensor", "sv", pb.tag("sv"))
    for ci in (0, 1):
        hb = cb(ctx, "headbc_b")[:, ci * 128:(ci + 1) * 128]
        t = pb.op("tensor", MM(P[0][:, 0:8], hb, AB("ldb", 0, 8, p1=16), True, True))
        pb.wt("vector", "st", t)
        pb.op("vector", CP(AF_("srow2", 80, 88), P[0][:, 0:8]), bar=True)
        pb.op("vector", TT(AF_("srow2", 80, 88), AF_("srow2", 80, 88),
                           AF_("srow2", (16 if ci == 0 else 24), (24 if ci == 0 else 32)),
                           ALU.mult), bar=True)
        pb.op("vector", RED(AF_("srow2", 88 + ci, 89 + ci), AF_("srow2", 80, 88), ALU.add),
              bar=True)
        pb.wt("tensor", "sv", pb.tag("sv"))
    for ci in (0, 1):
        pb.op("vector", SCANOP(AF_("sc1", ci * 1024, (ci + 1) * 1024),
                               AF_("dA", ci * 1024, (ci + 1) * 1024),
                               AF_("uf", ci * 1024, (ci + 1) * 1024),
                               AF_("srow2", 88 + ci, 89 + ci)), bar=True)
    if "sc2" in taps:
        pb.wt("sync", "sv", pb.tag("sv"))
        pb.dma("sync", DMA(tap["sc2"][:, :], AF_("sc1")))
    for ci in (0, 1):
        pb.op("vector", TT(AF_("uf", ci * 1024, (ci + 1) * 1024),
                           AF_("sc1", ci * 1024, (ci + 1) * 1024),
                           AF_("Cf", ci * 1024, (ci + 1) * 1024), ALU.mult), bar=True)
    t_yp = pb.tag("sv")
    pb.wt("tensor", "sv", t_yp)
    for h in (0, 1):
        for ci in (0, 1):
            t = pb.op("tensor", MM(P[2][0:16, 0:512],
                                   cf(ctx, "ysum_f")[:, ci * 16:(ci + 1) * 16],
                                   AF_("uf", ci * 1024 + h * 512, ci * 1024 + h * 512 + 512),
                                   ci == 0, ci == 1))
        pb.wt("vector", "st", t)
        pb.op("vector", CP(s3(h * 512, h * 512 + 512), P[2][0:16, 0:512]), bar=True)
        pb.wt("tensor", "sv", pb.tag("sv"))
    if "ypr" in taps:
        pb.wt("sync", "sv", pb.tag("sv"))
        pb.dma("sync", DMA(tap["ypr"][:, :], AF_("uf")))
    if "yscan" in taps:
        pb.wt("sync", "sv", pb.tag("sv"))
        pb.dma("sync", DMA(tap["yscan"][:, :], s3(0, 1024)))
    tyb = pb.op("vector", CP(AB("ldb", 0, 1024, p1=16), s3(0, 1024)), bar=True)

    twh = pb.dma("gpsimd", DMA(AB("wh2ib", 0, 4096, p1=16), EI["wh2i"][:, :]), sem="wsc")
    pb.wt("tensor", "wsc", twh)
    pb.wt("tensor", "sv", tyb)
    for half in (0, 1):
        for mc in range(32):
            buf = "wa" if mc % 2 == 0 else "wb"
            rec = stream_w(mc, buf)
            w = AB(buf)
            for kc in range(16):
                pb.op("tensor", MM(P[0][:, 0:512], w[:, kc * 128:(kc + 1) * 128],
                                   AB("xn2", kc * 1024 + half * 512,
                                      kc * 1024 + half * 512 + 512),
                                   kc == 0, kc == 15),
                      inc=(None if kc < 15 else "auto"))
            t = pb.op("tensor", MM(P[1][:, 0:512], AB("wh2ib", mc * 128, (mc + 1) * 128, p1=16),
                                   AB("ldb", half * 512, half * 512 + 512, p1=16),
                                   True, True))
            ring.done(rec, t)
            pb.wt("scalar", "st", t)
            tsc = pb.op("scalar", ACT(AB("zwork", 0, 512), P[0][:, 0:512], AF.Silu))
            pb.wt("vector", "st", t)
            pb.op("vector", CP(AB("zwork", 512, 1024), P[1][:, 0:512]))
            pb.wt("vector", "ss", tsc)
            pb.wt("vector", "sv", pb.tag("sv"))
            tv = pb.op("vector", TT(AB("yg", mc * 512, (mc + 1) * 512),
                                    AB("zwork", 0, 512), AB("zwork", 512, 1024), ALU.mult),
                       bar=True)
            pb.wt("tensor", "sv", tv)
        t_yg = pb.tag("sv")
        pb.wt("tensor", "sv", t_yg)
        for m in range(16):
            rec1 = ring.stream(DMA(AB("wa"), EI["w_outT"][:, m * 4096:m * 4096 + 2048]), 0)
            rec2 = ring.stream(DMA(AB("wb"), EI["w_outT"][:, m * 4096 + 2048:(m + 1) * 4096]), 1)
            ring.consume(rec1, 0)
            ring.consume(rec2, 1)
            for kc in range(32):
                w = AB("wa") if kc < 16 else AB("wb")
                kk = kc % 16
                t = pb.op("tensor", MM(P[2][:, 0:512], w[:, kk * 128:(kk + 1) * 128],
                                       AB("yg", kc * 512, (kc + 1) * 512), kc == 0, kc == 31),
                          inc=(None if kc not in (15, 31) else "auto"))
                if kc == 15:
                    ring.done(rec1, t)
            ring.done(rec2, t)
            pb.wt("vector", "st", t)
            xc = xs[:, m * 1024 + half * 512:m * 1024 + half * 512 + 512]
            tv = pb.op("vector", TT(xc, xc, P[2][:, 0:512], ALU.add), bar=True)
            pb.wt("tensor", "sv", tv)
    ctx.x2_tag = pb.tag("sv")
    if "x2" in taps:
        pb.wt("sync", "sv", ctx.x2_tag)
        for c in range(NCH):
            pb.dma("sync", DMA(tap["x2"][c * 128:(c + 1) * 128, :],
                               xs[:, c * 1024:(c + 1) * 1024]))


def emit_output(ctx):
    pb, xs, AF_ = ctx.pb, ctx.xs, ctx.AF
    pt = ctx.pt
    src_tag = ctx.x3_tag if ctx.phases >= 3 else ctx.x1_tag
    pb.wt("tensor", "sv", src_tag)
    AF_.alloc("otile", 2048, at=0) if "otile" not in AF_.map else None
    for rt in range(8):
        for c in range(NCH):
            t = pb.op("tensor", TRS(pt[:, :], xs[:, c * 1024 + rt * 128:c * 1024 + (rt + 1) * 128],
                                    cf(ctx, "ident_f")))
            pb.wt("vector", "st", t)
            tv = pb.op("vector", CP(AF_("otile", c * 128, (c + 1) * 128), pt[:, :]), bar=True)
            pb.wt("tensor", "sv", tv)
        pb.wt("sync", "sv", pb.tag("sv"))
        pb.dma("sync", DMA(ctx.out_t[rt * 128:(rt + 1) * 128, :], AF_("otile")), sem="ot")
        if rt < 7:
            pb.wt("vector", "ot", pb.tag("ot"))
    ctx.out_tag = pb.tag("ot")


def emit_final_wait(ctx):
    pb = ctx.pb
    for s in ("dq", "dg", "dsc", "wsa", "wsb", "wsc", "pz", "ls",
              "ps0", "ps1", "gsA", "xg0", "xg1", "sx0", "sx1", "ot"):
        pb.wt("sync", s, pb.tag(s))
    for s in list(pb.cnt):
        if s.startswith("cc_") and pb.cnt[s]:
            pb.wt("sync", s, 1)



def _prep_lhsT(w):
    K, M = w.shape
    kc, mc = K // 128, M // 128
    return np.ascontiguousarray(
        w.reshape(kc, 128, mc, 128).transpose(1, 2, 0, 3).reshape(128, mc * kc * 128),
        dtype=np.float32)


def make_inputs(inp):
    x = np.asarray(inp["L0_attn_x"], np.float32).reshape(NCORE * NTOK, D)
    wg_ = np.asarray(inp["L0_ffn_w_gate"], np.float32)
    wsu_ = np.asarray(inp["L0_ffn_w_shared_up"], np.float32)
    wsd_ = np.asarray(inp["L0_ffn_w_shared_down"], np.float32)
    wup_ = np.asarray(inp["L0_ffn_w_up"], np.float32)
    wdn_ = np.asarray(inp["L0_ffn_w_down"], np.float32)
    win_ = np.asarray(inp["L0_attn_W_in"], np.float32)
    wout_ = np.asarray(inp["L0_attn_W_out"], np.float32)
    wh2i_ = np.asarray(inp["L0_attn_W_h2i"], np.float32)
    wconv_ = np.asarray(inp["L0_attn_w_conv"], np.float32).reshape(4096, 4)
    wnorm_ = np.asarray(inp["L0_attn_w_norm"], np.float32)
    dtb_ = np.asarray(inp["L0_attn_dt_bias"], np.float32)
    alog_ = np.asarray(inp["L0_attn_A_log"], np.float32)

    wg_l = np.ascontiguousarray(wg_.reshape(16, 128, 64).transpose(1, 0, 2).reshape(128, 1024))
    wsu_l = _prep_lhsT(wsu_)
    wsd_l = _prep_lhsT(wsd_)
    WT = win_.T
    WTa = np.concatenate([WT[:, :8192], WT[:, 8192:8208],
                          np.zeros((2048, 112), np.float32),
                          WT[:, 8208:8464], WT[:, 8464:8720]], axis=1)
    win_l = _prep_lhsT(WTa)
    wout_l = _prep_lhsT(np.ascontiguousarray(wout_.T))
    wh2i_l = np.ascontiguousarray(wh2i_.T)
    wconv_l = np.ascontiguousarray(wconv_.reshape(32, 128, 4).transpose(1, 0, 2).reshape(128, 128))
    wnorm_l = np.ascontiguousarray(wnorm_.reshape(16, 128).T)

    consts0 = {
        "ident_f": np.eye(128, dtype=np.float32),
        "ident_b": np.eye(128, dtype=np.float32).astype(BF),
        "ones_col_b": np.full((128, 1), 1.0, BF),
        "ones_row_b": np.full((1, 128), 1.0, BF),
        "ones_row_f": np.full((1, 128), 1.0, np.float32),
        "iota_col": np.arange(128, dtype=np.float32).reshape(128, 1),
        "iota64": np.broadcast_to(np.arange(64, dtype=np.float32), (128, 64)).copy(),
        "ones_col_f": np.full((128, 1), 1.0, np.float32),
        "eps_ln": np.full((128, 1), LN_EPS, np.float32),
        "eps_rms": np.full((128, 1), RMS_EPS, np.float32),
        "one_colf": np.full((128, 1), 1.0, np.float32),
        "tri_excl_b": (np.arange(128)[:, None] < np.arange(128)[None, :]).astype(BF),
    }
    xm = np.zeros((32, 128, 16), np.float32)
    for ci in range(32):
        xm[ci, :, ci // 2] = 1.0 / 256.0
    consts0["xmean_sel_b"] = np.ascontiguousarray(
        xm.transpose(1, 0, 2).reshape(128, 512)).astype(BF)
    hbm_ = np.zeros((2, 16, 128), np.float32)
    for ci2 in range(2):
        for p_ in range(128):
            hbm_[ci2, ci2 * 8 + p_ // 16, p_] = 1.0
    consts0["headbc_b"] = np.ascontiguousarray(
        hbm_.transpose(1, 0, 2).reshape(16, 256)).astype(BF)
    ys = np.zeros((2, 128, 16), np.float32)
    for ci2 in range(2):
        for p_ in range(128):
            ys[ci2, p_, ci2 * 8 + p_ // 16] = 1.0
    consts0["ysum_f"] = np.ascontiguousarray(ys.transpose(1, 0, 2).reshape(128, 32))

    in_maps = []
    for core in range(NCORE):
        m = {}
        m["x_fm"] = np.ascontiguousarray(x[core * NTOK:(core + 1) * NTOK].T)
        m["wg"] = wg_l
        m["wsu"] = wsu_l
        m["wsd"] = wsd_l
        up8 = np.stack([_prep_lhsT(wup_[core * 8 + i]) for i in range(8)])
        m["wup"] = np.ascontiguousarray(up8.reshape(8 * 128, 8192))
        dn8 = np.stack([np.ascontiguousarray(
            wdn_[core * 8 + i].reshape(2, 128, 2048).transpose(1, 0, 2).reshape(128, 4096))
            for i in range(8)])
        m["wdn"] = np.ascontiguousarray(dn8.reshape(8 * 128, 4096))
        m["w_inT"] = win_l
        m["w_outT"] = wout_l
        m["wh2i"] = wh2i_l
        m["wconv"] = wconv_l
        m["wnormc"] = wnorm_l
        m["dtb"] = dtb_.reshape(16, 1)
        m["alog"] = alog_.reshape(16, 1)
        m.update(consts0)
        sel = np.zeros((64, 8), np.float32)
        for i in range(8):
            sel[core * 8 + i, i] = 1.0
        m["sel_own_f"] = sel
        m["tok_base"] = np.full((128, 1), core * NTOK, np.float32)
        m["eg1024_col"] = np.full((128, 1), core * 8 * 1024.0, np.float32)
        suf = np.zeros((8, 8), np.float32)
        for j in range(8):
            for mm_ in range(8):
                if j < mm_ < core:
                    suf[j, mm_] = 1.0
        m["stitch_w"] = suf
        m["stitch_pick"] = (np.arange(8) < core).astype(np.float32).reshape(8, 1)
        hs = np.zeros((32, 4), np.float32)
        if core > 0:
            hs[(core - 1) * 4:(core - 1) * 4 + 4] = np.eye(4, dtype=np.float32)
        m["halo_selm"] = hs
        in_maps.append(m)
    return in_maps


_prog_cache = {}


def kernel(**inputs):
    taps = tuple(sorted(inputs.pop("_taps", ())))
    phases = inputs.pop("_phases", 99)
    key = (taps, phases)
    if key not in _prog_cache:
        _prog_cache[key] = build_program(taps=taps, phases=phases)
    nc = _prog_cache[key]
    in_maps = make_inputs(inputs)
    res = run_bass_kernel_spmd(nc, in_maps, core_ids=list(range(NCORE)))
    outs = [r["out_t"] for r in res.results]
    full = np.concatenate(outs, axis=0).reshape(1, NCORE * NTOK, D)
    if taps:
        tapout = {t: [r["tap_" + t] for r in res.results] for t in taps}
        return full, tapout
    return full
